# version 13
# baseline (speedup 1.0000x reference)
import numpy as np
import ml_dtypes
import concourse.bass as bass
import concourse.mybir as mybir
from concourse.bass_utils import run_bass_kernel_spmd
from concourse.ap import AP

F32 = mybir.dt.float32
BF16 = mybir.dt.bfloat16
I8 = mybir.dt.int8
AF = mybir.ActivationFunctionType
ALU = mybir.AluOpType
AX = mybir.AxisListType

D = 2048
T = 2048
NSEQ = 2
NCORES = 8
DT = D // 128
TT = T // 128
H0, E0 = 32, 64
H1, E1 = 16, 128
HID0 = 5120
HID1 = 8192
HT0 = HID0 // 128
HT1 = HID1 // 128
ESIZE = {F32: 4, BF16: 2, I8: 1}

ENG_SEMS = ("te", "ve", "se", "ge")


class Plan:
    def __init__(self):
        self.steps = []
        self.cnt = {}
        self.slotnames = {}
        self.pending = {}

    def v(self, sem):
        return self.cnt.get(sem, 0)

    def slot_sem(self, key):
        if key not in self.slotnames:
            self.slotnames[key] = f"dm{len(self.slotnames)}"
        return self.slotnames[key]

    OWN = {"tensor": "te", "vector": "ve", "scalar": "se", "gpsimd": "ge"}

    def op(self, eng, fn, waits=(), inc=None, dma=None):
        incs = []
        if dma is not None:
            self.cnt[dma] = self.cnt.get(dma, 0) + 16
            incs.append((dma, 16))
        if inc is None and dma is None and eng in self.OWN:
            inc = self.OWN[eng]
        if inc is not None:
            self.cnt[inc] = self.cnt.get(inc, 0) + 1
            incs.append((inc, 1))
        ws = [(s, v) for (s, v) in waits if v and v > 0]
        pend = self.pending.pop(eng, None)
        if pend:
            ws = pend + ws
        self.steps.append((eng, fn, ws, incs))

    def fence(self):
        snap = [(s, v) for s, v in self.cnt.items() if v > 0]
        for eng in ("tensor", "vector", "scalar", "gpsimd", "sync"):
            own = self.OWN.get(eng)
            self.pending[eng] = [(s, v) for s, v in snap if s != own]

    def emit(self, nc, sems, self_waits=False):
        with nc.Block() as blk:
            for eng_name, reg in (("tensor", blk.tensor), ("vector", blk.vector),
                                  ("scalar", blk.scalar), ("gpsimd", blk.gpsimd),
                                  ("sync", blk.sync)):
                def body(eng, eng_name=eng_name):
                    waited = {}
                    own = self.OWN.get(eng_name)
                    self_cnt = 0
                    for e, fn, ws, incs in self.steps:
                        if e != eng_name:
                            continue
                        for s, val in ws:
                            if waited.get(s, 0) >= val:
                                continue
                            eng.wait_ge(sems[s], val)
                            waited[s] = val
                        ins = fn(eng)
                        for s, amt in incs:
                            ins.then_inc(sems[s], amt)
                        if own is not None and any(s == own for s, _ in incs):
                            if self_cnt > 0 and (self_waits or
                                                 eng_name != "tensor"):
                                ins._wait_ge(sems[own], self_cnt)
                            self_cnt += 1
                reg(body)


class Ctx:
    def __init__(self, nc):
        self.nc = nc
        self.dram = {}
        self.sbuf = {}
        self.psum = {}
        self.pl = Plan()
        self.state = {}

    def dram_t(self, name, shape, dtype, kind="Internal"):
        h = self.nc.dram_tensor(name, list(shape), dtype, kind=kind)
        self.dram[name] = h.ap()
        return self.dram[name]

    def inline(self, name, arr):
        ap = self.nc.inline_tensor(np.ascontiguousarray(arr), name).ap()
        self.dram[name] = ap
        return ap

    def vw(self, arena, off_bytes, dtype, cols, parts=128, base=0):
        es = ESIZE[dtype]
        assert off_bytes % max(es, 4) == 0
        h = self.sbuf[arena] if dtype == BF16 else self.sbuf[arena].bitcast(dtype)
        o = off_bytes // es
        return h[base:base + parts, o:o + cols]

    def vwp(self, arena, off_bytes, dtype, parts, cols):
        return self.vw(arena, off_bytes, dtype, cols, parts)

    def used(self, key, sem):
        if sem is None:
            return
        d = self.state.setdefault(("use", key), {})
        if isinstance(sem, tuple):
            d[sem[0]] = max(d.get(sem[0], 0), sem[1])
        else:
            d[sem] = self.pl.v(sem)

    def use_wait(self, key):
        d = self.state.get(("use", key))
        return list(d.items()) if d else []

    def dma(self, eng, fn, slot, waits=()):
        sem = self.pl.slot_sem(slot)
        self.pl.op(eng, fn, waits=tuple(waits) + tuple(self.use_wait(slot)),
                   dma=sem)
        return (sem, self.pl.v(sem))


def dram_bcast(ap_row, parts):
    return AP(ap_row.tensor, ap_row.offset,
              [[0, parts]] + [list(x) for x in ap_row.ap[1:]])


def alloc_common(cx):
    nc = cx.nc
    for nm, cols in (("A1", 32768), ("A2", 32768), ("A3", 24576), ("A4", 6144)):
        cx.sbuf[nm] = nc.alloc_sbuf_tensor(nm, [128, cols], BF16)
    cx.psum["ps_s"] = nc.alloc_psum_tensor("ps_s", [128, 2048], F32)
    cx.psum["ps_tp"] = nc.alloc_psum_tensor("ps_tp", [128, 2048], BF16)
    cx.psum["ps_z"] = nc.alloc_psum_tensor("ps_z", [128, 512], F32)
    cx.psum["ps_x"] = nc.alloc_psum_tensor("ps_x", [128, 512], F32)
    cx.inline("c_id_bf", np.eye(128, dtype=ml_dtypes.bfloat16))
    cx.inline("c_ones128_f", np.ones((128, 1), dtype=np.float32))
    cx.inline("c_ones1_f", np.ones((1, 128), dtype=np.float32))
    cx.inline("c_onesb", np.ones((128, 1), dtype=ml_dtypes.bfloat16))
    cx.inline("c_id_f32", np.eye(128, dtype=np.float32))
    mk = np.triu(np.full((128, 128), -1e9, dtype=np.float32), 1)
    cx.inline("c_maskt", mk)
    cx.inline("c_onesrow_bf", np.ones((1, 128), dtype=ml_dtypes.bfloat16))
    cx.inline("c_ones2_bf", np.ones((2, 128), dtype=ml_dtypes.bfloat16))
    cx.dram_t("scal_bounce", [1, 16], F32)


def a4(cx, name):
    m = {
        "id_bf": ("A4", 0, BF16, 128, 128),
        "onesb": ("A4", 256, BF16, 1, 128),
        "ones128_f": ("A4", 512, F32, 1, 128),
        "ones1_f": ("A4", 1024, F32, 128, 1),
        "amc": ("A4", 1536, F32, 1, 128),
        "thrc": ("A4", 2048, F32, 1, 128),
        "nthrc": ("A4", 2560, F32, 1, 128),
        "mrow": ("A4", 3072, F32, 512, 1),
        "vrow": ("A4", 5120, F32, 512, 1),
        "rstdrow": ("A4", 7168, F32, 512, 1),
        "part": ("A4", 9216, F32, 48, 128),
        "id_f32": ("A4", 9472, F32, 128, 128),
        "sc_all": ("A4", 10048, F32, 144, 128),
        "gcols": ("A4", 10624, F32, 16, 128),
        "negm": ("A4", 10688, F32, 1, 128),
        "negm1": ("A4", 11840, F32, 1, 128),
        "sums1": ("A4", 11872, F32, 1, 128),
        "recipc1": ("A4", 11904, F32, 1, 128),
        "pmax0": ("A4", 11920, F32, 4, 128),
        "pmax1": ("A4", 11936, F32, 4, 128),
        "sums40": ("A4", 11200, F32, 4, 128),
        "sums41": ("A4", 11216, F32, 4, 128),
        "ngc": ("A4", 10752, F32, 1, 128),
        "bqc": ("A4", 10816, F32, 16, 128),
        "bkc": ("A4", 10880, F32, 16, 128),
        "boc": ("A4", 10944, F32, 16, 128),
        "sums": ("A4", 11008, F32, 1, 128),
        "recipc": ("A4", 11072, F32, 1, 128),
        "crow16": ("A4", 11136, F32, 1, 16),
        "maskt": ("A4", 11264, F32, 128, 128),
        "bfc": ("A4", 11776, F32, 1, 32),
        "onesrow_bf": ("A4", 11968, BF16, 128, 1),
        "ones2_bf": ("A4", 11968, BF16, 128, 2),
        "ccol16": ("A4", 12224, F32, 1, 16),
    }
    ar, off, dt_, cols, parts = m[name]
    return cx.vw(ar, off, dt_, cols, parts)


def load_consts(cx):
    tok = None
    for nm in ("id_bf", "onesb", "ones128_f", "ones1_f", "id_f32", "maskt",
               "ones2_bf"):
        srcap = cx.dram["c_" + nm]
        dst = a4(cx, nm)
        tok = cx.dma("sync", lambda e, d=dst, s=srcap: e.dma_start(out=d, in_=s),
                     slot="cst")
    cx.state["consts"] = tok



def emit_prep_transpose(cx, w_ap, rows, cols, scr_ap, ternarize, scale=1.0):
    pl, ps = cx.pl, cx.psum
    nrt, nct = rows // 128, cols // 128
    pw = (cx.vw("A3", 0, F32, cols), cx.vw("A3", 20480, F32, cols))
    gt = cx.vw("A1", 0, F32, cols)
    wtb = cx.vw("A1", 20480, BF16, cols)
    ev0 = cx.vw("A1", 31744, BF16, 2048)
    id_bf = a4(cx, "id_bf")
    thr_ve = 0
    if ternarize:
        part = a4(cx, "part")
        for rt in range(nrt):
            buf = pw[rt % 2]
            ld = cx.dma("sync", lambda e, b=buf, rt=rt: e.dma_start(
                out=b, in_=w_ap[rt * 128:(rt + 1) * 128, :]), slot=("pw", rt % 2))
            pl.op("vector", lambda e, b=buf, rt=rt: e.tensor_reduce(
                out=part[:, rt:rt + 1], in_=b, axis=AX.X, op=ALU.add,
                apply_absolute_value=True), waits=[ld], inc="ve")
            cx.used(("pw", rt % 2), "ve")
        red_ve = pl.v("ve")
        pl.op("tensor", lambda e, nrt=nrt: e.matmul(
            out=ps["ps_x"][0:1, 0:nrt], lhsT=a4(cx, "ones128_f"),
            rhs=part[:, 0:nrt], start=True, stop=True),
            waits=[("ve", red_ve), cx.state["consts"]] + cx.use_wait("ps_x"),
            inc="te")
        mm_te = pl.v("te")
        mrow = a4(cx, "mrow")
        pl.op("vector", lambda e, nrt=nrt: e.tensor_reduce(
            out=mrow[0:1, 0:1], in_=ps["ps_x"][0:1, 0:nrt], axis=AX.X, op=ALU.add),
            waits=[("te", mm_te)], inc="ve")
        cx.used("ps_x", "ve")
        pl.op("vector", lambda e, n=rows * cols: e.tensor_scalar(
            out=mrow[0:1, 0:1], in0=mrow[0:1, 0:1], scalar1=1.0 / n,
            scalar2=1e-5, op0=ALU.mult, op1=ALU.max), inc="ve")
        am_ve = pl.v("ve")
        amd = cx.dram["scal_bounce"]
        st = cx.dma("gpsimd", lambda e: e.dma_start(
            out=amd[0:1, 0:1], in_=mrow[0:1, 0:1]), slot="amd",
            waits=[("ve", am_ve)])
        amc = a4(cx, "amc")
        am_ld = cx.dma("gpsimd", lambda e, amc=amc: e.dma_start(
            out=amc, in_=dram_bcast(amd[0:1, 0:1], 128)), slot="amc",
            waits=[st])
        thrc, nthrc = a4(cx, "thrc"), a4(cx, "nthrc")
        pl.op("vector", lambda e, amc=amc, thrc=thrc: e.tensor_scalar(
            out=thrc, in0=amc, scalar1=0.7, scalar2=None, op0=ALU.mult),
            waits=[am_ld], inc="ve")
        pl.op("vector", lambda e, thrc=thrc, nthrc=nthrc: e.tensor_scalar(
            out=nthrc, in0=thrc, scalar1=-1.0, scalar2=None, op0=ALU.mult),
            inc="ve")
        cx.used("amc", "ve")
        thr_ve = pl.v("ve")

    for rt in range(nrt):
        buf = pw[rt % 2]
        ld = cx.dma("sync", lambda e, b=buf, rt=rt: e.dma_start(
            out=b, in_=w_ap[rt * 128:(rt + 1) * 128, :]), slot=("pw", rt % 2))
        if ternarize:
            amc, thrc, nthrc = a4(cx, "amc"), a4(cx, "thrc"), a4(cx, "nthrc")
            pl.op("vector", lambda e, b=buf, gt=gt: e.tensor_scalar(
                out=gt, in0=b, scalar1=thrc, scalar2=amc,
                op0=ALU.is_gt, op1=ALU.mult),
                waits=[ld, ("ve", thr_ve)] + cx.use_wait("gt"), inc="ve")
            pl.op("vector", lambda e, b=buf: e.tensor_scalar(
                out=wtb, in0=b, scalar1=nthrc, scalar2=amc,
                op0=ALU.is_lt, op1=ALU.mult), waits=cx.use_wait("wtb"), inc="ve")
            cx.used(("pw", rt % 2), "ve")
            pl.op("vector", lambda e: e.tensor_sub(out=wtb, in0=gt, in1=wtb),
                  inc="ve")
            cx.used("gt", "ve")
        elif scale == 1.0:
            pl.op("vector", lambda e, b=buf: e.tensor_copy(out=wtb, in_=b),
                  waits=[ld] + cx.use_wait("wtb"), inc="ve")
            cx.used(("pw", rt % 2), "ve")
        else:
            pl.op("vector", lambda e, b=buf, sc=scale: e.tensor_scalar(
                out=wtb, in0=b, scalar1=sc, scalar2=None, op0=ALU.mult),
                waits=[ld] + cx.use_wait("wtb"), inc="ve")
            cx.used(("pw", rt % 2), "ve")
        cast_ve = pl.v("ve")
        for g0 in range(0, nct, 16):
            gn = min(16, nct - g0)
            for j in range(gn):
                ct = g0 + j
                pl.op("tensor", lambda e, ct=ct, j=j: e.transpose(
                    out=ps["ps_tp"][:, j * 128:(j + 1) * 128],
                    in_=wtb[:, ct * 128:(ct + 1) * 128], identity=id_bf),
                    waits=[("ve", cast_ve), cx.state["consts"]] +
                          (cx.use_wait("ps_tp") if j == 0 else []), inc="te")
            tr_te = pl.v("te")
            pl.op("scalar", lambda e, gn=gn: e.activation(
                out=ev0[:, 0:gn * 128], in_=ps["ps_tp"][:, 0:gn * 128],
                func=AF.Copy), waits=[("te", tr_te)] + cx.use_wait("ev0"),
                inc="se")
            cx.used("ps_tp", "se")
            ev_se = pl.v("se")
            tok = cx.dma("gpsimd", lambda e, rt=rt, g0=g0, gn=gn: e.dma_start(
                out=scr_ap[rt, :, g0 * 128:(g0 + gn) * 128],
                in_=ev0[:, 0:gn * 128]), slot="ev0", waits=[("se", ev_se)])
            cx.used("ev0", tok)
        cx.used("wtb", "te")


def emit_prep_cast(cx, w_ap, rows, cols, scr_ap, dtype_in=F32):
    pl = cx.pl
    nrt = rows // 128
    assert cols <= 8192
    for rt in range(nrt):
        if dtype_in == I8:
            buf = cx.vw("A3", 0, I8, cols)
        else:
            buf = cx.vw("A3", 0, F32, cols)
        wtb = cx.vw("A1", 0, BF16, cols)
        ld = cx.dma("sync", lambda e, b=buf, rt=rt: e.dma_start(
            out=b, in_=w_ap[rt * 128:(rt + 1) * 128, :]), slot="pwc")
        pl.op("vector", lambda e, b=buf, w=wtb: e.tensor_copy(out=w, in_=b),
              waits=[ld] + cx.use_wait("wtbc"), inc="ve")
        cx.used("pwc", "ve")
        wv = pl.v("ve")
        tok = cx.dma("gpsimd", lambda e, rt=rt, w=wtb: e.dma_start(
            out=scr_ap[rt, :, 0:w.shape[1]], in_=w), slot="wtbc",
            waits=[("ve", wv)])
        cx.used("wtbc", tok)



def emit_ln_chunk(cx, tc, r_out, ych, ysq, y_ready=()):
    pl, ps = cx.pl, cx.psum
    for dt_ in range(DT):
        pl.op("scalar", lambda e, dt_=dt_: e.activation(
            out=ysq[:, dt_ * 512:(dt_ + 1) * 512],
            in_=ych[:, dt_ * 512:(dt_ + 1) * 512], func=AF.Square),
            waits=list(y_ready) + (cx.use_wait("lnysq") if dt_ == 0 else []),
            inc="se")
    casts_se = pl.v("se")
    ones_f = a4(cx, "ones128_f")
    onesb = a4(cx, "onesb")
    for dt_ in range(DT):
        pl.op("tensor", lambda e, dt_=dt_: e.matmul(
            out=ps["ps_z"][0:1, :], lhsT=ones_f,
            rhs=ych[:, dt_ * 512:(dt_ + 1) * 512],
            start=(dt_ == 0), stop=(dt_ == DT - 1)),
            waits=list(y_ready) + (cx.use_wait("ps_z") if dt_ == 0 else []),
            inc="te")
    for dt_ in range(DT):
        pl.op("tensor", lambda e, dt_=dt_: e.matmul(
            out=ps["ps_x"][0:1, :], lhsT=onesb,
            rhs=ysq[:, dt_ * 512:(dt_ + 1) * 512],
            start=(dt_ == 0), stop=(dt_ == DT - 1)),
            waits=[("se", casts_se)] + (cx.use_wait("ps_x") if dt_ == 0 else []),
            inc="te")
    stats_te = pl.v("te")
    mrow, vrow, rstd = a4(cx, "mrow"), a4(cx, "vrow"), a4(cx, "rstdrow")
    pl.op("vector", lambda e: e.tensor_scalar(
        out=mrow, in0=ps["ps_z"][0:1, :], scalar1=1.0 / D, scalar2=None,
        op0=ALU.mult), waits=[("te", stats_te)], inc="ve")
    pl.op("vector", lambda e: e.tensor_scalar(
        out=vrow, in0=ps["ps_x"][0:1, :], scalar1=1.0 / D, scalar2=None,
        op0=ALU.mult), inc="ve")
    tnrow = cx.vw("A3", 22528, F32, 512, 1)
    pl.op("vector", lambda e, tnrow=tnrow: e.tensor_mul(
        out=tnrow, in0=mrow, in1=mrow), inc="ve")
    pl.op("vector", lambda e, tnrow=tnrow: e.tensor_sub(
        out=vrow, in0=vrow, in1=tnrow), inc="ve")
    pl.op("vector", lambda e: e.tensor_scalar(
        out=vrow, in0=vrow, scalar1=1e-5, scalar2=None, op0=ALU.add), inc="ve")
    rows_ve = pl.v("ve")
    pl.op("scalar", lambda e, tnrow=tnrow: e.activation(
        out=tnrow, in_=vrow, func=AF.Sqrt), waits=[("ve", rows_ve)], inc="se")
    sq_se = pl.v("se")
    pl.op("vector", lambda e, tnrow=tnrow: e.reciprocal(out=rstd, in_=tnrow),
          waits=[("se", sq_se)], inc="ve")
    rstd_ve = pl.v("ve")
    ones1 = a4(cx, "ones1_f")
    pl.op("tensor", lambda e: e.matmul(out=ps["ps_z"][:, :], lhsT=ones1,
                                       rhs=mrow, start=True, stop=True),
          waits=[("ve", rstd_ve)], inc="te")
    pl.op("tensor", lambda e: e.matmul(out=ps["ps_x"][:, :], lhsT=ones1,
                                       rhs=rstd, start=True, stop=True), inc="te")
    bc_te = pl.v("te")
    for dt_ in range(DT):
        sl = dt_ % 2
        rn = cx.vw("A3", 26624 + sl * 2048, F32, 512)
        tn = cx.vw("A3", 24576, F32, 512)
        pl.op("vector", lambda e, dt_=dt_, tn=tn: e.tensor_sub(
            out=tn, in0=ych[:, dt_ * 512:(dt_ + 1) * 512], in1=ps["ps_z"][:, :]),
            waits=[("te", bc_te)], inc="ve")
        pl.op("vector", lambda e, rn=rn, tn=tn: e.tensor_mul(
            out=rn, in0=tn, in1=ps["ps_x"][:, :]),
            waits=cx.use_wait(("lnrn", sl)), inc="ve")
        w_ve = pl.v("ve")
        tok = cx.dma("gpsimd", lambda e, dt_=dt_, rn=rn: e.dma_start(
            out=r_out[dt_ * 128:(dt_ + 1) * 128, tc * 512:(tc + 1) * 512], in_=rn),
            slot=("lnrn", sl), waits=[("ve", w_ve)])
        cx.used(("lnrn", sl), tok)
    cx.used("ps_z", "ve")
    cx.used("ps_x", "ve")
    cx.used("lnysq", "te")



def emit_l0f_seq(cx, s, r_in, r_out):
    pl, ps = cx.pl, cx.psum
    wfcT = cx.dram["wfcT_s"]
    wprT = cx.dram["wprT_s"]
    hsp = cx.dram["h_sp"]
    xb = cx.vw("A1", 0, BF16, DT * T)
    ych = cx.vw("A1", 16384, F32, 8192)
    ysq = cx.vw("A3", 30720, BF16, 8192)
    wcs = (cx.vw("A3", 0, BF16, 2048), cx.vw("A3", 4096, BF16, 2048))
    xfs = (cx.vw("A3", 16384, F32, 2048), cx.vw("A3", 24576, F32, 2048))
    hstg = (cx.vw("A3", 32768, BF16, 2048), cx.vw("A3", 36864, BF16, 2048))
    evr = cx.vw("A3", 40960, BF16, 512)
    wdq = (cx.vw("A3", 12288, BF16, 1280), cx.vw("A3", 14848, BF16, 1280),
           cx.vw("A3", 17408, BF16, 1280), cx.vw("A3", 19968, BF16, 1280))
    for dt_ in range(DT):
        buf = xfs[dt_ % 2]
        ld = cx.dma("sync", lambda e, b=buf, dt_=dt_: e.dma_start(
            out=b, in_=r_in[dt_ * 128:(dt_ + 1) * 128, :]), slot=("xf", dt_ % 2))
        pl.op("vector", lambda e, b=buf, dt_=dt_: e.tensor_copy(
            out=xb[:, dt_ * 2048:(dt_ + 1) * 2048], in_=b),
            waits=[ld] + (cx.use_wait("xbf") if dt_ == 0 else []), inc="ve")
        cx.used(("xf", dt_ % 2), "ve")
    xb_ve = pl.v("ve")
    for ht in range(HT0):
        wb = wcs[ht % 2]
        wld = cx.dma("sync", lambda e, wb=wb, ht=ht: e.dma_start(
            out=wb, in_=wfcT[ht, :, :]), slot=("wcf", ht % 2))
        fb = hstg[ht % 2]
        for tc in range(4):
            slot = tc % 2
            pp = ps["ps_s"][:, slot * 512:(slot + 1) * 512]
            for db in range(DT):
                pl.op("tensor", lambda e, pp=pp, wb=wb, db=db, tc=tc: e.matmul(
                    out=pp, lhsT=wb[:, db * 128:(db + 1) * 128],
                    rhs=xb[:, db * 2048 + tc * 512: db * 2048 + (tc + 1) * 512],
                    start=(db == 0), stop=(db == DT - 1)),
                    waits=[wld, ("ve", xb_ve)] +
                          (cx.use_wait(("pss", slot)) if db == 0 else []),
                    inc="te")
            mm_te = pl.v("te")
            pl.op("scalar", lambda e, pp=pp, evr=evr: e.activation(
                out=evr, in_=pp, func=AF.Relu),
                waits=[("te", mm_te)] + cx.use_wait("evr"), inc="se")
            r_se = pl.v("se")
            pl.op("scalar", lambda e, tc=tc, evr=evr, fb=fb: e.activation(
                out=fb[:, tc * 512:(tc + 1) * 512], in_=evr, func=AF.Square),
                waits=[("se", r_se)] +
                      (cx.use_wait(("hstg", ht % 2)) if tc == 0 else []),
                inc="se")
            cx.used(("pss", slot), "se")
            cx.used("evr", "se")
        h_se = pl.v("se")
        for tc in range(4):
            tok = cx.dma("gpsimd", lambda e, ht=ht, tc=tc, fb=fb: e.dma_start(
                out=hsp[tc, ht], in_=fb[:, tc * 512:(tc + 1) * 512]),
                slot=("hstg", ht % 2), waits=[("se", h_se)])
        cx.used(("hstg", ht % 2), tok)
        cx.used(("wcf", ht % 2), "te")
    cx.used("xbf", "te")
    cx.pl.fence()
    hq = cx.vw("A2", 0, BF16, HT0 * 512)
    for q in range(4):
        t0 = q * 512
        fld = cx.dma("sync", lambda e, q=q: e.dma_start(
            out=hq.rearrange("p (a b) -> p a b", b=512),
            in_=hsp[q].rearrange("ht p b -> p ht b")), slot="hq")
        for dt_ in range(DT):
            slot = dt_ % 2
            toks = {}
            for g in range(4):
                gi = dt_ * 4 + g
                b = wdq[gi % 4]
                toks[g] = cx.dma("sync", lambda e, b=b, dt_=dt_, g=g: e.dma_start(
                    out=b, in_=wprT[dt_, :, g * 1280:(g + 1) * 1280]),
                    slot=("wdq", gi % 4))
            pp = ps["ps_s"][:, (2 + slot) * 512:(3 + slot) * 512]
            for hb in range(HT0):
                g = hb // 10
                gi = dt_ * 4 + g
                wb = wdq[gi % 4]
                co = (hb % 10) * 128
                pl.op("tensor", lambda e, pp=pp, wb=wb, co=co, hb=hb: e.matmul(
                    out=pp, lhsT=wb[:, co:co + 128],
                    rhs=hq[:, hb * 512:(hb + 1) * 512],
                    start=(hb == 0), stop=(hb == HT0 - 1)),
                    waits=[toks[g], fld] +
                          (cx.use_wait(("psp", slot)) if hb == 0 else []),
                    inc="te")
                if hb % 10 == 9:
                    cx.used(("wdq", gi % 4), "te")
            mm_te = pl.v("te")
            rb = cx.vw("A3", 43008 + (dt_ % 2) * 2048, F32, 512)
            rld = cx.dma("sync", lambda e, rb=rb, dt_=dt_, t0=t0: e.dma_start(
                out=rb, in_=r_in[dt_ * 128:(dt_ + 1) * 128, t0:t0 + 512]),
                slot=("rr", dt_ % 2))
            pl.op("vector", lambda e, pp=pp, rb=rb, dt_=dt_: e.tensor_add(
                out=ych[:, dt_ * 512:(dt_ + 1) * 512], in0=pp, in1=rb),
                waits=[("te", mm_te), rld] +
                      (cx.use_wait("lnych") if dt_ == 0 else []), inc="ve")
            cx.used(("rr", dt_ % 2), "ve")
            cx.used(("psp", slot), "ve")
        y_ve = pl.v("ve")
        emit_ln_chunk(cx, q, r_out, ych, ysq, y_ready=[("ve", y_ve)])
        cx.used("lnych", "te")
        cx.used("hq", "te")
    cx.state[("use", "hq")] = cx.state.get(("use", "hq"), {})


def emit_l0f(cx, seqs=(0, 1), prep=True, rin="r{s}_in", rout="r{s}_out"):
    if prep:
        emit_prep_transpose(cx, cx.dram["L0_ffn_w_fc"], HID0, D,
                            cx.dram["wfcT_s"], ternarize=True)
        cx.pl.fence()
        emit_prep_transpose(cx, cx.dram["L0_ffn_w_proj"], D, HID0,
                            cx.dram["wprT_s"], ternarize=True)
        cx.pl.fence()
    for s in seqs:
        emit_l0f_seq(cx, s, cx.dram[rin.format(s=s)], cx.dram[rout.format(s=s)])
        cx.pl.fence()



def emit_l1f_prep(cx):
    pl, ps = cx.pl, cx.psum
    for wname, scr_name, nr, nc_, nblk in (
            ("L1_ffn_w_gate", "wguT_s", DT, HID1, HT1),
            ("L1_ffn_w_up", "wguT_s", DT, HID1, HT1),
            ("L1_ffn_w_down", "wdT_s", HT1, D, DT)):
        w_ap = cx.dram[wname]
        scr = cx.dram[scr_name]
        coff = 2048 if wname == "L1_ffn_w_up" else 0
        buf8 = cx.vw("A3", 0, I8, nc_)
        wtb = cx.vw("A1", 0, BF16, nc_)
        for rt in range(nr):
            ld = cx.dma("sync", lambda e, b=buf8, rt=rt, wa=w_ap: e.dma_start(
                out=b, in_=wa[rt * 128:(rt + 1) * 128, :]), slot="pwc")
            pl.op("vector", lambda e, b=buf8, w=wtb: e.tensor_copy(out=w, in_=b),
                  waits=[ld] + cx.use_wait("wtbc"), inc="ve")
            cx.used("pwc", "ve")
            wv = pl.v("ve")
            for g0 in range(0, nblk, 16):
                gn = min(16, nblk - g0)
                out_ap = scr[g0:g0 + gn, :,
                             coff + rt * 128:coff + (rt + 1) * 128]
                out_r = out_ap.rearrange("ct dmod hm -> dmod ct hm")
                in_r = wtb[:, g0 * 128:(g0 + gn) * 128].rearrange(
                    "dmod (ct hm) -> dmod ct hm", hm=128)
                tok = cx.dma("gpsimd", lambda e, o=out_r, i=in_r: e.dma_start(
                    out=o, in_=i), slot="wtbc", waits=[("ve", wv)])
                cx.used("wtbc", tok)
    for sname, dest_off, nblk in (("L1_ffn_s_gate", 0, HT1),
                                  ("L1_ffn_s_up", 256, HT1),
                                  ("L1_ffn_s_down", 512, DT)):
        s_ap = cx.dram[sname]
        srcv = s_ap.rearrange("(a b) -> a b", b=128)
        stmp = cx.vw("A3", 0, F32, 128, parts=nblk)
        ld = cx.dma("sync", lambda e, d=stmp, s=srcv: e.dma_start(out=d, in_=s),
                    slot="pwc")
        idf = a4(cx, "id_f32")
        pl.op("tensor", lambda e, d=stmp, idf=idf, nblk=nblk: e.transpose(
            out=ps["ps_z"][0:128, 0:nblk], in_=d,
            identity=idf[0:nblk, 0:nblk]),
            waits=[ld, cx.state["consts"]] + cx.use_wait("ps_z"), inc="te")
        tr = pl.v("te")
        dst = cx.dram["scales_s"]
        sc_sb = cx.vw("A3", 8192, F32, 128, parts=128)
        pl.op("scalar", lambda e, d=sc_sb, nblk=nblk: e.activation(
            out=d[:, 0:nblk], in_=ps["ps_z"][0:128, 0:nblk], func=AF.Copy),
            waits=[("te", tr)] + cx.use_wait("scst"), inc="se")
        cx.used("ps_z", "se")
        sv = pl.v("se")
        tok = cx.dma("gpsimd", lambda e, d=dst, o=dest_off, n=nblk, s=sc_sb:
                     e.dma_start(out=d[:, o // 4:o // 4 + n], in_=s[:, 0:n]),
                     slot="scst", waits=[("se", sv)])
        cx.used("pwc", "te")
        cx.used("scst", tok)


def emit_l1f_seq(cx, s, r_in, r_out):
    pl, ps = cx.pl, cx.psum
    wguT, wdT = cx.dram["wguT_s"], cx.dram["wdT_s"]
    fsp = cx.dram["fuse_sp"]
    xb = cx.vw("A1", 0, BF16, DT * T)
    ych = cx.vw("A1", 16384, F32, 8192)
    wdh = (cx.vw("A1", 49152, BF16, 2048), cx.vw("A1", 53248, BF16, 2048),
           cx.vw("A1", 57344, BF16, 2048), cx.vw("A1", 61440, BF16, 2048))
    ysq = cx.vw("A3", 30720, BF16, 8192)
    wgu = (cx.vw("A3", 0, BF16, 4096), cx.vw("A3", 8192, BF16, 4096))
    xfs = (cx.vw("A3", 16384, F32, 2048), cx.vw("A3", 24576, F32, 2048))
    fstg = (cx.vw("A3", 32768, BF16, 2048), cx.vw("A3", 36864, BF16, 2048))
    silb = cx.vw("A3", 40960, BF16, 512)
    sc_all = a4(cx, "sc_all")
    scld = cx.dma("sync", lambda e, d=sc_all: e.dma_start(
        out=d, in_=cx.dram["scales_s"][:, 0:144]), slot="scld")
    for dt_ in range(DT):
        buf = xfs[dt_ % 2]
        ld = cx.dma("sync", lambda e, b=buf, dt_=dt_: e.dma_start(
            out=b, in_=r_in[dt_ * 128:(dt_ + 1) * 128, :]), slot=("xf", dt_ % 2))
        pl.op("vector", lambda e, b=buf, dt_=dt_: e.tensor_copy(
            out=xb[:, dt_ * 2048:(dt_ + 1) * 2048], in_=b),
            waits=[ld] + (cx.use_wait("xbf") if dt_ == 0 else []), inc="ve")
        cx.used(("xf", dt_ % 2), "ve")
    xb_ve = pl.v("ve")
    for ht in range(HT1):
        gub = wgu[ht % 2]
        gld = cx.dma("sync", lambda e, b=gub, ht=ht: e.dma_start(
            out=b, in_=wguT[ht, :, :]), slot=("wg", ht % 2))
        fb = fstg[ht % 2]
        for tc in range(4):
            ppg = ps["ps_s"][:, 0:512]
            ppu = ps["ps_s"][:, 512:1024]
            for db in range(DT):
                pl.op("tensor", lambda e, pp=ppg, b=gub, db=db, tc=tc: e.matmul(
                    out=pp, lhsT=b[:, db * 128:(db + 1) * 128],
                    rhs=xb[:, db * 2048 + tc * 512: db * 2048 + (tc + 1) * 512],
                    start=(db == 0), stop=(db == DT - 1)),
                    waits=[gld, ("ve", xb_ve)] +
                          (cx.use_wait("psg") if db == 0 else []), inc="te")
            for db in range(DT):
                pl.op("tensor", lambda e, pp=ppu, b=gub, db=db, tc=tc: e.matmul(
                    out=pp, lhsT=b[:, 2048 + db * 128:2048 + (db + 1) * 128],
                    rhs=xb[:, db * 2048 + tc * 512: db * 2048 + (tc + 1) * 512],
                    start=(db == 0), stop=(db == DT - 1)),
                    waits=[gld, ("ve", xb_ve)] +
                          (cx.use_wait("psu") if db == 0 else []), inc="te")
            mm_te = pl.v("te")
            pl.op("scalar", lambda e, pp=ppg, ht=ht: e.activation(
                out=silb, in_=pp, func=AF.Silu,
                scale=sc_all[:, ht:ht + 1]),
                waits=[("te", mm_te), scld] + cx.use_wait("silb"), inc="se")
            cx.used("psg", "se")
            sil_se = pl.v("se")
            pl.op("vector", lambda e, pp=ppu, ht=ht, fb=fb, tc=tc:
                  e.scalar_tensor_tensor(
                out=fb[:, tc * 512:(tc + 1) * 512], in0=pp,
                scalar=sc_all[:, 64 + ht:65 + ht], in1=silb,
                op0=ALU.mult, op1=ALU.mult),
                waits=[("se", sil_se)] +
                      (cx.use_wait(("fstg", ht % 2)) if tc == 0 else []),
                inc="ve")
            cx.used("psu", "ve")
            cx.used("silb", "ve")
        f_ve = pl.v("ve")
        for tc in range(4):
            tok = cx.dma("gpsimd", lambda e, ht=ht, tc=tc, fb=fb: e.dma_start(
                out=fsp[tc, ht], in_=fb[:, tc * 512:(tc + 1) * 512]),
                slot=("fstg", ht % 2), waits=[("ve", f_ve)])
        cx.used(("fstg", ht % 2), tok)
        cx.used(("wg", ht % 2), "te")
    cx.used("xbf", "te")
    cx.pl.fence()
    fuseq = cx.vw("A2", 0, BF16, HT1 * 512)
    for q in range(4):
        t0 = q * 512
        fld = cx.dma("sync", lambda e, q=q: e.dma_start(
            out=fuseq.rearrange("p (a b) -> p a b", b=512),
            in_=fsp[q].rearrange("ht p b -> p ht b")), slot="fq")
        for dt_ in range(DT):
            toks = {}
            for g in range(4):
                gi = dt_ * 4 + g
                b = wdh[gi % 4]
                toks[g] = cx.dma("sync", lambda e, b=b, dt_=dt_, g=g: e.dma_start(
                    out=b, in_=wdT[dt_, :, g * 2048:(g + 1) * 2048]),
                    slot=("wd", gi % 4))
            pp = ps["ps_s"][:, 1024 + (dt_ % 2) * 512:1536 + (dt_ % 2) * 512]
            for hb in range(HT1):
                g = hb // 16
                gi = dt_ * 4 + g
                b = wdh[gi % 4]
                co = (hb % 16) * 128
                pl.op("tensor", lambda e, pp=pp, b=b, co=co, hb=hb: e.matmul(
                    out=pp, lhsT=b[:, co:co + 128],
                    rhs=fuseq[:, hb * 512:(hb + 1) * 512],
                    start=(hb == 0), stop=(hb == HT1 - 1)),
                    waits=[toks[g], fld] +
                          (cx.use_wait(("psd", dt_ % 2)) if hb == 0 else []),
                    inc="te")
                if hb % 16 == 15:
                    cx.used(("wd", gi % 4), "te")
            mm_te = pl.v("te")
            rb = cx.vw("A3", 16384 + (dt_ % 2) * 2048, F32, 512)
            rld = cx.dma("sync", lambda e, rb=rb, dt_=dt_, t0=t0: e.dma_start(
                out=rb, in_=r_in[dt_ * 128:(dt_ + 1) * 128, t0:t0 + 512]),
                slot=("xf", dt_ % 2))
            pl.op("vector", lambda e, pp=pp, rb=rb, dt_=dt_: e.scalar_tensor_tensor(
                out=ych[:, dt_ * 512:(dt_ + 1) * 512], in0=pp,
                scalar=sc_all[:, 128 + dt_:129 + dt_], in1=rb,
                op0=ALU.mult, op1=ALU.add),
                waits=[("te", mm_te), rld] +
                      (cx.use_wait("lnych") if dt_ == 0 else []), inc="ve")
            cx.used(("xf", dt_ % 2), "ve")
            cx.used(("psd", dt_ % 2), "ve")
        y_ve = pl.v("ve")
        emit_ln_chunk(cx, q, r_out, ych, ysq, y_ready=[("ve", y_ve)])
        cx.used("lnych", "te")
        cx.used("fq", "te")
    cx.used("scld", "ve")


def emit_l1f(cx, seqs=(0, 1), prep=True, rin="r{s}_in", rout="r{s}_out"):
    if prep:
        emit_l1f_prep(cx)
        cx.pl.fence()
    for s in seqs:
        emit_l1f_seq(cx, s, cx.dram[rin.format(s=s)], cx.dram[rout.format(s=s)])
        cx.pl.fence()


def build_l1f_launch():
    cx = Ctx(bass.Bass("TRN2", target_bir_lowering=False))
    cx.dram_t("L1_ffn_w_gate", [D, HID1], I8, kind="ExternalInput")
    cx.dram_t("L1_ffn_w_up", [D, HID1], I8, kind="ExternalInput")
    cx.dram_t("L1_ffn_w_down", [HID1, D], I8, kind="ExternalInput")
    cx.dram_t("L1_ffn_s_gate", [HID1], F32, kind="ExternalInput")
    cx.dram_t("L1_ffn_s_up", [HID1], F32, kind="ExternalInput")
    cx.dram_t("L1_ffn_s_down", [D], F32, kind="ExternalInput")
    for s in range(NSEQ):
        cx.dram_t(f"r{s}_in", [D, T], F32, kind="ExternalInput")
        cx.dram_t(f"r{s}_out", [D, T], F32, kind="ExternalOutput")
    cx.dram_t("wguT_s", [HT1, 128, 2 * D], BF16)
    cx.dram_t("wdT_s", [DT, 128, HID1], BF16)
    cx.dram_t("scales_s", [128, 256], F32)
    cx.dram_t("fuse_sp", [4, HT1, 128, 512], BF16)
    alloc_common(cx)
    load_consts(cx)
    emit_l1f(cx)
    return finalize(cx)



ATTN_SCALE1 = float(np.sqrt(np.float32(E1)))


def emit_l1a_prep(cx):
    pl, ps = cx.pl, cx.psum
    for wname, scr in (("L1_attn_W_Q", "wq1_s"), ("L1_attn_W_K", "wk1_s"),
                       ("L1_attn_W_V", "wv1_s")):
        w = cx.dram[wname].rearrange("h (dd w) e -> (h dd) (w e)", w=64)
        sc = cx.dram[scr].rearrange("h (dd w) e -> (h dd) (w e)", w=64)
        sc3 = AP(sc.tensor, sc.offset,
                 [[8192, 4], [8192 * 4 if False else 0, 0]]) if False else None
        scv = cx.dram[scr].rearrange("h (dd w) e -> (h dd) (w e)", w=64)
        scv = scv.rearrange("(a p) c -> a p c", p=128)
        wv = w.rearrange("(a p) c -> a p c", p=128)
        for rt in range(4):
            buf = cx.vw("A3", 0, F32, 8192)
            wtb = cx.vw("A1", 0, BF16, 8192)
            ld = cx.dma("sync", lambda e, b=buf, rt=rt, wv=wv: e.dma_start(
                out=b, in_=wv[rt]), slot="pwc")
            pl.op("vector", lambda e, b=buf, w_=wtb: e.tensor_copy(out=w_, in_=b),
                  waits=[ld] + cx.use_wait("wtbc"), inc="ve")
            cx.used("pwc", "ve")
            vv = pl.v("ve")
            tok = cx.dma("gpsimd", lambda e, rt=rt, w_=wtb, scv=scv: e.dma_start(
                out=scv[rt], in_=w_), slot="wtbc", waits=[("ve", vv)])
            cx.used("wtbc", tok)
    wo = cx.dram["L1_attn_W_O"].rearrange("h e d -> (h e) d")
    wov = wo.rearrange("(a p) c -> a p c", p=128)
    scov = cx.dram["wo1_s"].rearrange("h e d -> (h e) d").rearrange(
        "(a p) c -> a p c", p=128)
    for rt in range(16):
        buf = cx.vw("A3", 0, F32, 2048)
        wtb = cx.vw("A1", 0, BF16, 2048)
        ld = cx.dma("sync", lambda e, b=buf, rt=rt, wov=wov: e.dma_start(
            out=b, in_=wov[rt]), slot="pwc")
        pl.op("vector", lambda e, b=buf, w_=wtb: e.tensor_copy(out=w_, in_=b),
              waits=[ld] + cx.use_wait("wtbc"), inc="ve")
        cx.used("pwc", "ve")
        vv = pl.v("ve")
        tok = cx.dma("gpsimd", lambda e, rt=rt, w_=wtb, scov=scov: e.dma_start(
            out=scov[rt], in_=w_), slot="wtbc", waits=[("ve", vv)])
        cx.used("wtbc", tok)
    idf = a4(cx, "id_f32")
    for bname, dest, scale in (("L1_attn_b_Q", "bqc", 1.0 / ATTN_SCALE1),
                               ("L1_attn_b_K", "bkc", 1.0),
                               ("L1_attn_b_O", "boc", 1.0)):
        b_ap = cx.dram[bname]
        if bname == "L1_attn_b_O":
            b_ap = b_ap.rearrange("(a b) -> a b", b=128)
        btmp = cx.vw("A3", 0, F32, 128, parts=16)
        ld = cx.dma("sync", lambda e, d=btmp, s=b_ap: e.dma_start(out=d, in_=s),
                    slot="pwc")
        pl.op("tensor", lambda e, d=btmp, idf=idf: e.transpose(
            out=ps["ps_z"][0:128, 0:16], in_=d, identity=idf[0:16, 0:16]),
            waits=[ld, cx.state["consts"]] + cx.use_wait("ps_z"), inc="te")
        tr = pl.v("te")
        cx.used("pwc", "te")
        pl.op("scalar", lambda e, dest=dest, sc=scale: e.activation(
            out=a4(cx, dest), in_=ps["ps_z"][0:128, 0:16], func=AF.Identity,
            scale=sc), waits=[("te", tr)], inc="se")
        cx.used("ps_z", "se")


def emit_l1a_seq(cx, s, r_in, r_out):
    pl, ps = cx.pl, cx.psum
    q_dram, v_dram, z_dram = cx.dram["q_sp"], cx.dram["v_sp"], cx.dram["z_sp"]
    xb = cx.vw("A1", 0, BF16, DT * T)
    kT = cx.vw("A2", 0, BF16, DT * T)
    wcols = (cx.vw("A3", 0, BF16, 2048), cx.vw("A3", 4096, BF16, 2048))
    xfs = (cx.vw("A3", 30720, F32, 512), cx.vw("A3", 32768, F32, 512))
    qsp = cx.vw("A3", 34816, BF16, 2048)
    bvrow = cx.vw("A3", 38912, F32, 2048, parts=1)
    wvall = cx.vw("A3", 8192, BF16, 16 * 512)
    for dt_ in range(DT):
        for tc in range(4):
            buf = xfs[tc % 2]
            ld = cx.dma("sync", lambda e, b=buf, dt_=dt_, tc=tc: e.dma_start(
                out=b, in_=r_in[dt_ * 128:(dt_ + 1) * 128,
                                tc * 512:(tc + 1) * 512]), slot=("xf", tc % 2))
            pl.op("vector", lambda e, b=buf, dt_=dt_, tc=tc: e.tensor_copy(
                out=xb[:, dt_ * 2048 + tc * 512: dt_ * 2048 + (tc + 1) * 512],
                in_=b), waits=[ld], inc="ve")
            cx.used(("xf", tc % 2), "ve")
    xb_ve = pl.v("ve")
    bvld = cx.dma("sync", lambda e: e.dma_start(
        out=bvrow, in_=cx.dram["L1_attn_b_V"].rearrange("a b -> (a b)").rearrange(
            "(a c) -> a c", a=1)), slot="bv")
    for which, scr, bias, scale in (("k", "wk1_s", "bkc", None),
                                    ("q", "wq1_s", "bqc", 1.0 / ATTN_SCALE1)):
        for et in range(H1):
            wb = wcols[et % 2]
            wld = cx.dma("sync", lambda e, wb=wb, et=et, scr=scr: e.dma_start(
                out=wb.rearrange("p (db em) -> p db em", em=128),
                in_=cx.dram[scr][et].rearrange(
                    "(db dmod) em -> dmod db em", dmod=128)),
                slot=("wc", et % 2))
            for tc in range(4):
                pp = ps["ps_s"][:, (tc % 2) * 512:(tc % 2 + 1) * 512]
                for db in range(DT):
                    pl.op("tensor", lambda e, pp=pp, wb=wb, db=db, tc=tc: e.matmul(
                        out=pp, lhsT=wb[:, db * 128:(db + 1) * 128],
                        rhs=xb[:, db * 2048 + tc * 512: db * 2048 + (tc + 1) * 512],
                        start=(db == 0), stop=(db == DT - 1)),
                        waits=[wld, ("ve", xb_ve)] +
                              (cx.use_wait(("psqk", tc % 2)) if db == 0 else []),
                        inc="te")
                mm_te = pl.v("te")
                if which == "k":
                    pl.op("scalar", lambda e, pp=pp, et=et, tc=tc: e.activation(
                        out=kT[:, et * 2048 + tc * 512: et * 2048 + (tc + 1) * 512],
                        in_=pp, func=AF.Identity, bias=a4(cx, "bkc")[:, et:et + 1]),
                        waits=[("te", mm_te), cx.state["consts"]], inc="se")
                else:
                    pl.op("scalar", lambda e, pp=pp, tc=tc, et=et: e.activation(
                        out=qsp[:, tc * 512:(tc + 1) * 512], in_=pp,
                        func=AF.Identity,
                        bias=a4(cx, "bqc")[:, et:et + 1], scale=1.0 / ATTN_SCALE1),
                        waits=[("te", mm_te)] + cx.use_wait("qspb"), inc="se")
                cx.used(("psqk", tc % 2), "se")
            cx.used(("wc", et % 2), "te")
            if which == "q":
                sv = pl.v("se")
                tok = cx.dma("gpsimd", lambda e, et=et: e.dma_start(
                    out=q_dram[et], in_=qsp), slot="qspst", waits=[("se", sv)])
                cx.used("qspb", tok)
                cx.used(("qspb", 0), tok)
                cx.used(("qspb", 1), tok)
    for ec in range(4):
        wvtok = None
        for db in range(DT):
            wvtok = cx.dma("sync", lambda e, db=db, ec=ec: e.dma_start(
                out=wvall[:, db * 512:(db + 1) * 512].rearrange(
                    "p (h em) -> p h em", em=128),
                in_=cx.dram["wv1_s"][4 * ec:4 * ec + 4, db * 128:(db + 1) * 128, :]
                .rearrange("h dmod em -> dmod h em")), slot="wvall")
        for tt in range(TT):
            pp = ps["ps_s"][:, 1024 + (tt % 2) * 512:1536 + (tt % 2) * 512]
            for db in range(DT):
                pl.op("tensor", lambda e, pp=pp, db=db, tt=tt: e.matmul(
                    out=pp, lhsT=xb[:, db * 2048 + tt * 128: db * 2048 + tt * 128 + 128],
                    rhs=wvall[:, db * 512:(db + 1) * 512],
                    start=(db == 0), stop=False),
                    waits=[("ve", xb_ve), wvtok] +
                          (cx.use_wait(("psv", tt % 2)) if db == 0 else []),
                    inc="te")
            pl.op("tensor", lambda e, pp=pp, ec=ec: e.matmul(
                out=pp, lhsT=a4(cx, "ones1_f"),
                rhs=bvrow[0:1, ec * 512:(ec + 1) * 512],
                start=False, stop=True), waits=[bvld], inc="te")
            mm_te = pl.v("te")
            pl.op("scalar", lambda e, pp=pp, tt=tt: e.activation(
                out=qsp[:, (tt % 2) * 512:(tt % 2) * 512 + 512], in_=pp,
                func=AF.Copy),
                waits=[("te", mm_te)] + cx.use_wait(("qspb", tt % 2)), inc="se")
            cx.used(("psv", tt % 2), "se")
            sv = pl.v("se")
            tok = cx.dma("gpsimd", lambda e, tt=tt, ec=ec: e.dma_start(
                out=v_dram[tt, :, ec * 512:(ec + 1) * 512],
                in_=qsp[:, (tt % 2) * 512:(tt % 2) * 512 + 512]),
                slot=("qspst", tt % 2), waits=[("se", sv)])
            cx.used(("qspb", tt % 2), tok)
        cx.used("wvall", "te")
    cx.pl.fence()
    v_sb = cx.vw("A1", 0, BF16, DT * T)
    qhs = (cx.vw("A3", 0, BF16, 2048), cx.vw("A3", 4096, BF16, 2048))
    pns = (cx.vw("A3", 8192, BF16, 2048), cx.vw("A3", 12288, BF16, 2048))
    ptbuf = cx.vw("A3", 16384, BF16, 8192)
    pt3 = ptbuf.rearrange("p (a b) -> p a b", b=512)
    zstage = cx.vw("A3", 34816, BF16, 512)
    vtok = None
    for tt in range(TT):
        vtok = cx.dma("sync", lambda e, tt=tt: e.dma_start(
            out=v_sb[:, tt * 2048:(tt + 1) * 2048], in_=v_dram[tt]), slot="vld")
    negms = (a4(cx, "negm"), a4(cx, "negm1"))
    sumss = (a4(cx, "sums"), a4(cx, "sums1"))
    recips = (a4(cx, "recipc"), a4(cx, "recipc1"))
    maskt = a4(cx, "maskt")
    for h in range(H1):
        qh = qhs[h % 2]
        qld = cx.dma("sync", lambda e, h=h, qh=qh: e.dma_start(
            out=qh, in_=q_dram[h]), slot=("qh", h % 2))
        pending = [None]

        def flush_l1(cx=cx, pl=pl, ps=ps, pending=pending):
            if pending[0] is None:
                return
            pn, ti, pnorm_ve = pending[0]
            pending[0] = None
            ti4 = ti % 4
            for tj in range(ti + 1):
                half = tj % 8
                pl.op("tensor", lambda e, pn=pn, tj=tj, half=half: e.transpose(
                    out=ps["ps_tp"][:, half * 128:(half + 1) * 128],
                    in_=pn[:, tj * 128:(tj + 1) * 128],
                    identity=a4(cx, "id_bf")),
                    waits=[("ve", pnorm_ve)] +
                          (cx.use_wait("pstp") if half == 0 else []),
                    inc="te")
                if half == 7 or tj == ti:
                    t_te = pl.v("te")
                    n = half + 1
                    first = tj - half
                    pl.op("vector", lambda e, n=n, first=first, ti4=ti4:
                          e.tensor_copy(
                        out=pt3[:, first:first + n,
                                ti4 * 128:ti4 * 128 + 128],
                        in_=ps["ps_tp"][:, 0:n * 128].rearrange(
                            "p (a b) -> p a b", b=128)),
                        waits=[("te", t_te)], inc="ve")
                    cx.used("pstp", "ve")
            cx.used(("pn", ti4 % 2), "te")
            if ti4 == 3:
                ic = ti // 4
                ptb_ve = pl.v("ve")
                for tj in range(4 * (ic + 1)):
                    o = max(0, (tj - 4 * ic) * 128)
                    pl.op("tensor", lambda e, tj=tj, o=o, h=h, ic=ic: e.matmul(
                        out=ps["ps_z"][:, o:512],
                        lhsT=v_sb[:, tj * 2048 + h * 128:
                                  tj * 2048 + h * 128 + 128],
                        rhs=pt3[:, tj, o:512],
                        start=(tj == 0), stop=(tj == 4 * (ic + 1) - 1)),
                        waits=[("ve", ptb_ve), vtok] +
                              (cx.use_wait("psz") if tj == 0 else []), inc="te")
                    cx.used("ptbuf", "te")
                av_te = pl.v("te")
                pl.op("scalar", lambda e: e.activation(
                    out=zstage, in_=ps["ps_z"][:, 0:512], func=AF.Copy),
                    waits=[("te", av_te)] + cx.use_wait("zst"), inc="se")
                cx.used("psz", "se")
                zv = pl.v("se")
                tok = cx.dma("gpsimd", lambda e, h=h, ic=ic: e.dma_start(
                    out=z_dram[h, :, ic * 512:(ic + 1) * 512], in_=zstage),
                    slot="zst", waits=[("se", zv)])
                cx.used("zst", tok)

        for ic in range(4):
            for ti4 in range(4):
                ti = 4 * ic + ti4
                ji = 128 * (ti + 1)
                pn = pns[ti4 % 2]
                par = ti % 2
                negm, sums, recipc = negms[par], sumss[par], recips[par]
                nb = (ji + 511) // 512
                pmax = a4(cx, "pmax0") if par == 0 else a4(cx, "pmax1")
                for jc in range(nb):
                    n = min(512, ji - jc * 512)
                    pl.op("tensor", lambda e, qh=qh, h=h, jc=jc, n=n, ti=ti:
                          e.matmul(
                        out=ps["ps_s"][:, jc * 512:jc * 512 + n],
                        lhsT=qh[:, ti * 128:(ti + 1) * 128],
                        rhs=kT[:, h * 2048 + jc * 512: h * 2048 + jc * 512 + n],
                        start=True, stop=True),
                        waits=[qld] + cx.use_wait(("psS", jc)), inc="te")
                    s_te = pl.v("te")
                    if jc == nb - 1:
                        dji = ji - 128
                        pl.op("vector", lambda e, dji=dji: e.tensor_tensor(
                            out=ps["ps_s"][:, dji:dji + 128],
                            in0=ps["ps_s"][:, dji:dji + 128],
                            in1=maskt, op=ALU.add),
                            waits=[("te", s_te), cx.state["consts"]] +
                                  cx.use_wait(("negm", par)), inc="ve")
                    pl.op("vector", lambda e, jc=jc, n=n, pmax=pmax:
                          e.tensor_reduce(
                        out=pmax[:, jc:jc + 1],
                        in_=ps["ps_s"][:, jc * 512:jc * 512 + n],
                        axis=AX.X, op=ALU.max),
                        waits=[("te", s_te)] +
                              (cx.use_wait(("sums", par)) if jc == 0 else []),
                        inc="ve")
                pl.op("vector", lambda e, negm=negm, pmax=pmax, nb=nb:
                      e.tensor_reduce(
                    out=negm, in_=pmax[:, 0:nb], axis=AX.X, op=ALU.max,
                    negate=True), waits=cx.use_wait(("negm", par)), inc="ve")
                mx_ve = pl.v("ve")
                sums4 = a4(cx, "sums40") if par == 0 else a4(cx, "sums41")
                for jc in range(nb):
                    n = min(512, ji - jc * 512)
                    pl.op("scalar", lambda e, pn=pn, jc=jc, n=n, negm=negm,
                          sums4=sums4: e.activation(
                        out=pn[:, jc * 512:jc * 512 + n],
                        in_=ps["ps_s"][:, jc * 512:jc * 512 + n], func=AF.Exp,
                        bias=negm, accum_out=sums4[:, jc:jc + 1]),
                        waits=[("ve", mx_ve)] +
                              (cx.use_wait(("pn", ti4 % 2)) +
                               cx.use_wait(("sums4", par)) if jc == 0 else []),
                        inc="se")
                    cx.used(("psS", jc), "se")
                e_se = pl.v("se")
                cx.used(("negm", par), "se")
                pl.op("vector", lambda e, sums=sums, sums4=sums4, nb=nb:
                      e.tensor_reduce(out=sums, in_=sums4[:, 0:nb], axis=AX.X,
                                      op=ALU.add),
                      waits=[("se", e_se)] + cx.use_wait(("sums", par)),
                      inc="ve")
                cx.used(("sums4", par), "ve")
                pl.op("vector", lambda e, recipc=recipc, sums=sums:
                      e.reciprocal(out=recipc, in_=sums),
                      waits=cx.use_wait(("recip", par)), inc="ve")
                cx.used(("sums", par), "ve")
                pl.op("vector", lambda e, pn=pn, ji=ji, recipc=recipc:
                      e.tensor_scalar(
                    out=pn[:, 0:ji], in0=pn[:, 0:ji], scalar1=recipc,
                    scalar2=None, op0=ALU.mult), inc="ve")
                cx.used(("recip", par), "ve")
                pnorm_ve = pl.v("ve")
                flush_l1()
                pending[0] = (pn, ti, pnorm_ve)
        flush_l1()
        cx.used(("qh", h % 2), "te")
    cx.pl.fence()
    z_sb = cx.vw("A1", 0, BF16, DT * T)
    ych = cx.vw("A2", 0, F32, 8192)
    ysq = cx.vw("A2", 32768, BF16, 8192)
    wocol = (cx.vw("A3", 0, BF16, 2048), cx.vw("A3", 4096, BF16, 2048),
             cx.vw("A3", 8192, BF16, 2048))
    ztok = None
    for et in range(H1):
        ztok = cx.dma("sync", lambda e, et=et: e.dma_start(
            out=z_sb[:, et * 2048:(et + 1) * 2048], in_=z_dram[et]), slot="zl2")
    for tc in range(4):
        for dt_ in range(DT):
            wb = wocol[dt_ % 3]
            wld = cx.dma("sync", lambda e, wb=wb, dt_=dt_: e.dma_start(
                out=wb.rearrange("p (h dm) -> p h dm", dm=128),
                in_=cx.dram["wo1_s"][:, :, dt_ * 128:(dt_ + 1) * 128]
                .rearrange("h emod dm -> emod h dm")), slot=("woc", dt_ % 3))
            pp = ps["ps_s"][:, (dt_ % 2) * 512:(dt_ % 2 + 1) * 512]
            for ec in range(H1):
                pl.op("tensor", lambda e, pp=pp, wb=wb, ec=ec, tc=tc: e.matmul(
                    out=pp, lhsT=wb[:, ec * 128:(ec + 1) * 128],
                    rhs=z_sb[:, ec * 2048 + tc * 512: ec * 2048 + (tc + 1) * 512],
                    start=(ec == 0), stop=(ec == H1 - 1)),
                    waits=[wld, ztok] +
                          (cx.use_wait(("psO", dt_ % 2)) if ec == 0 else []),
                    inc="te")
            mm_te = pl.v("te")
            cx.used(("woc", dt_ % 3), "te")
            rb = cx.vw("A3", 30720, F32, 512) if dt_ % 2 == 0 else \
                cx.vw("A3", 32768, F32, 512)
            rld = cx.dma("sync", lambda e, rb=rb, dt_=dt_, tc=tc: e.dma_start(
                out=rb, in_=r_in[dt_ * 128:(dt_ + 1) * 128,
                                 tc * 512:(tc + 1) * 512]), slot=("xf", dt_ % 2))
            pl.op("vector", lambda e, pp=pp, rb=rb, dt_=dt_: e.scalar_tensor_tensor(
                out=ych[:, dt_ * 512:(dt_ + 1) * 512], in0=pp,
                scalar=a4(cx, "boc")[:, dt_:dt_ + 1], in1=rb,
                op0=ALU.add, op1=ALU.add),
                waits=[("te", mm_te), rld] +
                      (cx.use_wait("lnych") if dt_ == 0 else []), inc="ve")
            cx.used(("xf", dt_ % 2), "ve")
            cx.used(("psO", dt_ % 2), "ve")
        y_ve = pl.v("ve")
        emit_ln_chunk(cx, tc, r_out, ych, ysq, y_ready=[("ve", y_ve)])
        cx.used("lnych", "te")


def emit_l1a(cx, seqs=(0, 1), prep=True, rin="r{s}_in", rout="r{s}_out"):
    if prep:
        emit_l1a_prep(cx)
        cx.pl.fence()
    for s in seqs:
        emit_l1a_seq(cx, s, cx.dram[rin.format(s=s)], cx.dram[rout.format(s=s)])
        cx.pl.fence()


def build_l1a_launch():
    cx = Ctx(bass.Bass("TRN2", target_bir_lowering=False))
    cx.dram_t("L1_attn_W_Q", [H1, D, E1], F32, kind="ExternalInput")
    cx.dram_t("L1_attn_W_K", [H1, D, E1], F32, kind="ExternalInput")
    cx.dram_t("L1_attn_W_V", [H1, D, E1], F32, kind="ExternalInput")
    cx.dram_t("L1_attn_W_O", [H1, E1, D], F32, kind="ExternalInput")
    cx.dram_t("L1_attn_b_Q", [H1, E1], F32, kind="ExternalInput")
    cx.dram_t("L1_attn_b_K", [H1, E1], F32, kind="ExternalInput")
    cx.dram_t("L1_attn_b_V", [H1, E1], F32, kind="ExternalInput")
    cx.dram_t("L1_attn_b_O", [D], F32, kind="ExternalInput")
    for s in range(NSEQ):
        cx.dram_t(f"r{s}_in", [D, T], F32, kind="ExternalInput")
        cx.dram_t(f"r{s}_out", [D, T], F32, kind="ExternalOutput")
    cx.dram_t("wq1_s", [H1, D, E1], BF16)
    cx.dram_t("wk1_s", [H1, D, E1], BF16)
    cx.dram_t("wv1_s", [H1, D, E1], BF16)
    cx.dram_t("wo1_s", [H1, E1, D], BF16)
    cx.dram_t("q_sp", [H1, 128, T], BF16)
    cx.dram_t("v_sp", [TT, 128, T], BF16)
    cx.dram_t("z_sp", [H1, 128, T], BF16)
    alloc_common(cx)
    load_consts(cx)
    emit_l1a(cx)
    return finalize(cx)



def emit_l0a_prep(cx):
    pl, ps = cx.pl, cx.psum
    emit_prep_transpose(cx, cx.dram["L0_attn_Wq"], D, D, cx.dram["wq0_s"],
                        ternarize=False, scale=0.125)
    cx.pl.fence()
    emit_prep_transpose(cx, cx.dram["L0_attn_Wk"], D, D, cx.dram["wk0_s"],
                        ternarize=False)
    cx.pl.fence()
    emit_prep_transpose(cx, cx.dram["L0_attn_Wv"], D, D, cx.dram["wv0_s"],
                        ternarize=False)
    cx.pl.fence()
    emit_prep_transpose(cx, cx.dram["L0_attn_Wo"], D, D, cx.dram["wo0_s"],
                        ternarize=False)
    cx.pl.fence()
    wfl = cx.vw("A3", 0, F32, 2048, parts=32)
    idf = a4(cx, "id_f32")
    ld = cx.dma("sync", lambda e, d=wfl: e.dma_start(
        out=d, in_=cx.dram["L0_attn_Wf"]), slot="pwc")
    for ct in range(16):
        pl.op("tensor", lambda e, ct=ct, d=wfl: e.transpose(
            out=ps["ps_z"][0:128, 0:32], in_=d[:, ct * 128:(ct + 1) * 128],
            identity=idf[0:32, 0:32]),
            waits=[ld, cx.state["consts"]] + cx.use_wait("ps_z"), inc="te")
        tr = pl.v("te")
        ev = cx.vw("A3", 20480, F32, 32)
        pl.op("scalar", lambda e, ev=ev: e.activation(
            out=ev, in_=ps["ps_z"][0:128, 0:32], func=AF.Copy),
            waits=[("te", tr)] + cx.use_wait("wfev"), inc="se")
        cx.used("ps_z", "se")
        sv = pl.v("se")
        tok = cx.dma("gpsimd", lambda e, ct=ct, ev=ev: e.dma_start(
            out=cx.dram["wf0_s"][ct], in_=ev), slot="wfev", waits=[("se", sv)])
        cx.used("wfev", tok)
    cx.used("pwc", "te")
    bfv = cx.dram["L0_attn_bf"].rearrange("(a b) -> a b", b=1)
    cx.dma("sync", lambda e: e.dma_start(out=a4(cx, "bfc"), in_=bfv), slot="bfc")


def emit_l0a_seq(cx, s, r_in, r_out):
    pl, ps = cx.pl, cx.psum
    q_dram, v_dram, z_dram = cx.dram["q_sp"], cx.dram["v_sp"], cx.dram["z_sp0"]
    cpos = cx.dram["c_sp"]
    cs2 = cx.dram["cs2_sp"]
    xb = cx.vw("A1", 0, BF16, DT * T)
    kT = cx.vw("A2", 0, BF16, DT * T)
    xfs = (cx.vw("A3", 0, F32, 2048), cx.vw("A3", 8192, F32, 2048))
    zf = cx.vw("A3", 16384, F32, 2048, parts=32)
    u1 = cx.vw("A3", 24576, F32, 2048, parts=32)
    u2 = cx.vw("A3", 32768, F32, 2048, parts=32)
    wfT = cx.vw("A3", 45056, F32, 512)
    wfld = cx.dma("sync", lambda e, d=wfT: e.dma_start(
        out=d.rearrange("p (a b) -> p a b", b=32),
        in_=cx.dram["wf0_s"].rearrange("a p b -> p a b")), slot="wfld")
    bfld = cx.dma("sync", lambda e: e.dma_start(
        out=a4(cx, "bfc"),
        in_=cx.dram["L0_attn_bf"].rearrange("(a b) -> a b", b=1)), slot="bfc")
    for dt_ in range(DT):
        buf = xfs[dt_ % 2]
        ld = cx.dma("sync", lambda e, b=buf, dt_=dt_: e.dma_start(
            out=b, in_=r_in[dt_ * 128:(dt_ + 1) * 128, :]), slot=("xf", dt_ % 2))
        pl.op("vector", lambda e, b=buf, dt_=dt_: e.tensor_copy(
            out=xb[:, dt_ * 2048:(dt_ + 1) * 2048], in_=b),
            waits=[ld], inc="ve")
        cx.used(("xf", dt_ % 2), "ve")
        for tc in range(4):
            pl.op("tensor", lambda e, b=buf, dt_=dt_, tc=tc: e.matmul(
                out=ps["ps_s"][0:32, tc * 512:(tc + 1) * 512],
                lhsT=wfT[:, dt_ * 32:(dt_ + 1) * 32],
                rhs=b[:, tc * 512:(tc + 1) * 512],
                start=(dt_ == 0), stop=(dt_ == DT - 1)),
                waits=[ld, wfld] +
                      (cx.use_wait(("psf", tc)) if dt_ == 0 else []), inc="te")
        cx.used(("xf", dt_ % 2), "te")
    f_te = pl.v("te")
    xb_ve = pl.v("ve")
    pl.op("scalar", lambda e: e.activation(
        out=zf, in_=ps["ps_s"][0:32, 0:2048], func=AF.Identity,
        bias=a4(cx, "bfc")), waits=[("te", f_te), bfld], inc="se")
    for tc in range(4):
        cx.used(("psf", tc), "se")
    pl.op("scalar", lambda e: e.activation(out=u1, in_=zf, func=AF.Abs),
          inc="se")
    pl.op("scalar", lambda e: e.activation(out=u2, in_=u1, func=AF.Exp,
                                           scale=-1.0), inc="se")
    pl.op("scalar", lambda e: e.activation(out=u1, in_=u2, func=AF.Ln,
                                           bias=1.0), inc="se")
    ls_se = pl.v("se")
    pl.op("vector", lambda e: e.tensor_scalar(
        out=u2, in0=zf, scalar1=0.0, scalar2=None, op0=ALU.min),
        waits=[("se", ls_se)], inc="ve")
    pl.op("vector", lambda e: e.tensor_sub(out=u1, in0=u1, in1=u2), inc="ve")
    pl.op("vector", lambda e: e.memset(zf, 1.0), inc="ve")
    pl.op("vector", lambda e: e.tensor_tensor_scan(
        out=u2, data0=zf, data1=u1, initial=0.0, op0=ALU.mult, op1=ALU.add),
        inc="ve")
    sc_ve = pl.v("ve")
    ctok = cx.dma("gpsimd", lambda e: e.dma_start(out=cpos, in_=u2),
                  slot="cst0", waits=[("ve", sc_ve)])
    cx.pl.fence()
    wcols = (cx.vw("A3", 0, BF16, 2048), cx.vw("A3", 4096, BF16, 2048))
    wvall = cx.vw("A3", 8192, BF16, 16 * 512)
    qsp = cx.vw("A3", 34816, BF16, 2048)
    for which, scr in (("k", "wk0_s"), ("q", "wq0_s")):
        for et in range(DT):
            wb = wcols[et % 2]
            wld = cx.dma("sync", lambda e, wb=wb, et=et, scr=scr: e.dma_start(
                out=wb, in_=cx.dram[scr][et]), slot=("wc", et % 2))
            for tc in range(4):
                pp = ps["ps_s"][:, (tc % 2) * 512:(tc % 2 + 1) * 512]
                for db in range(DT):
                    pl.op("tensor", lambda e, pp=pp, wb=wb, db=db, tc=tc: e.matmul(
                        out=pp, lhsT=wb[:, db * 128:(db + 1) * 128],
                        rhs=xb[:, db * 2048 + tc * 512: db * 2048 + (tc + 1) * 512],
                        start=(db == 0), stop=(db == DT - 1)),
                        waits=[wld, ("ve", xb_ve)] +
                              (cx.use_wait(("psqk", tc % 2)) if db == 0 else []),
                        inc="te")
                mm_te = pl.v("te")
                if which == "k":
                    pl.op("scalar", lambda e, pp=pp, et=et, tc=tc: e.activation(
                        out=kT[:, et * 2048 + tc * 512: et * 2048 + (tc + 1) * 512],
                        in_=pp, func=AF.Copy), waits=[("te", mm_te)], inc="se")
                else:
                    pl.op("scalar", lambda e, pp=pp, tc=tc: e.activation(
                        out=qsp[:, tc * 512:(tc + 1) * 512], in_=pp, func=AF.Copy),
                        waits=[("te", mm_te)] + cx.use_wait("qspb"), inc="se")
                cx.used(("psqk", tc % 2), "se")
            cx.used(("wc", et % 2), "te")
            if which == "q":
                sv = pl.v("se")
                tok = cx.dma("gpsimd", lambda e, et=et: e.dma_start(
                    out=q_dram[et], in_=qsp), slot="qspst", waits=[("se", sv)])
                cx.used("qspb", tok)
                cx.used(("qspb", 0), tok)
                cx.used(("qspb", 1), tok)
    for ec in range(4):
        wvtok = None
        for db in range(DT):
            wvtok = cx.dma("sync", lambda e, db=db, ec=ec: e.dma_start(
                out=wvall[:, db * 512:(db + 1) * 512].rearrange(
                    "p (a em) -> p a em", em=128),
                in_=cx.dram["wv0_s"][4 * ec:4 * ec + 4, :,
                                     db * 128:(db + 1) * 128]
                .rearrange("et dmod em -> dmod et em")), slot="wvall")
        for tt in range(TT):
            pp = ps["ps_s"][:, 1024 + (tt % 2) * 512:1536 + (tt % 2) * 512]
            for db in range(DT):
                pl.op("tensor", lambda e, pp=pp, db=db, tt=tt: e.matmul(
                    out=pp,
                    lhsT=xb[:, db * 2048 + tt * 128: db * 2048 + tt * 128 + 128],
                    rhs=wvall[:, db * 512:(db + 1) * 512],
                    start=(db == 0), stop=(db == DT - 1)),
                    waits=[("ve", xb_ve), wvtok] +
                          (cx.use_wait(("psv", tt % 2)) if db == 0 else []),
                    inc="te")
            mm_te = pl.v("te")
            pl.op("scalar", lambda e, pp=pp, tt=tt: e.activation(
                out=qsp[:, (tt % 2) * 512:(tt % 2) * 512 + 512], in_=pp,
                func=AF.Copy),
                waits=[("te", mm_te)] + cx.use_wait(("qspb", tt % 2)), inc="se")
            cx.used(("psv", tt % 2), "se")
            sv = pl.v("se")
            tok = cx.dma("gpsimd", lambda e, tt=tt, ec=ec: e.dma_start(
                out=v_dram[tt, :, ec * 512:(ec + 1) * 512],
                in_=qsp[:, (tt % 2) * 512:(tt % 2) * 512 + 512]),
                slot=("qspst", tt % 2), waits=[("se", sv)])
            cx.used(("qspb", tt % 2), tok)
        cx.used("wvall", "te")
    cx.pl.fence()
    v_sb = cx.vw("A1", 0, BF16, DT * T)
    qhs = (cx.vw("A3", 0, BF16, 2048, parts=64),
           cx.vw("A3", 0, BF16, 2048, parts=64, base=64))
    zstage = cx.vw("A3", 4096, BF16, 512, parts=64)
    pns = (cx.vw("A3", 8192, BF16, 2048), cx.vw("A3", 12288, BF16, 2048))
    ptbuf = cx.vw("A3", 16384, BF16, 8192)
    pt3 = ptbuf.rearrange("p (a b) -> p a b", b=512)
    crow16 = cx.vw("A3", 32768, F32, 2048, parts=16)
    crow16sh = cx.vw("A3", 40960, F32, 2048, parts=16)
    crowhi = cx.vw("A3", 32768, BF16, 2048, parts=16)
    crowlo = cx.vw("A3", 36864, BF16, 2048, parts=16)
    augrows = (cx.vw("A3", 32768, BF16, 2048, parts=2),
               cx.vw("A3", 36864, BF16, 2048, parts=2))
    vtok = None
    for tt in range(TT):
        vtok = cx.dma("sync", lambda e, tt=tt: e.dma_start(
            out=v_sb[:, tt * 2048:(tt + 1) * 2048], in_=v_dram[tt]), slot="vld")
    negms = (a4(cx, "negm"), a4(cx, "negm1"))
    sumss = (a4(cx, "sums"), a4(cx, "sums1"))
    recips = (a4(cx, "recipc"), a4(cx, "recipc1"))
    maskt = a4(cx, "maskt")
    ccol16 = a4(cx, "ccol16")
    onesrow = a4(cx, "onesrow_bf")
    ones2 = a4(cx, "ones2_bf")
    for h in range(H0):
        qh = qhs[h % 2]
        qld = cx.dma("sync", lambda e, h=h, qh=qh: e.dma_start(
            out=qh, in_=q_dram[h // 2][(h % 2) * 64:(h % 2) * 64 + 64, :]),
            slot=("qh", h % 2))
        c16ld = cx.dma("sync", lambda e, h=h: e.dma_start(
            out=crow16, in_=dram_bcast(cpos[h:h + 1, :], 16)), slot="c16",
            waits=[ctok] + cx.use_wait(("ar", 0)) + cx.use_wait(("ar", 1)))
        ccld = cx.dma("sync", lambda e, h=h: e.dma_start(
            out=ccol16, in_=AP(cpos.tensor, cpos.offset + h * 2048 + 127,
                               [[128, 16], [1, 1]]),
            allow_slow_non_contiguous=True), slot="ccol", waits=[ctok])
        pl.op("vector", lambda e: e.tensor_scalar(
            out=crow16sh, in0=crow16, scalar1=ccol16, scalar2=None,
            op0=ALU.subtract),
            waits=[c16ld, ccld] + cx.use_wait("c16sh") + cx.use_wait(("ar", 1)),
            inc="ve")
        pl.op("vector", lambda e: e.tensor_copy(out=crowhi, in_=crow16sh),
              waits=cx.use_wait("c16hi") + cx.use_wait(("ar", 0)), inc="ve")
        pl.op("vector", lambda e: e.tensor_sub(out=crow16sh, in0=crow16sh,
                                               in1=crowhi), inc="ve")
        pl.op("vector", lambda e: e.tensor_copy(out=crowlo, in_=crow16sh),
              waits=cx.use_wait("c16lo") + cx.use_wait(("ar", 1)), inc="ve")
        shv = pl.v("ve")
        cs2tok = cx.dma("gpsimd", lambda e, h=h: e.dma_start(
            out=cs2[0], in_=crowhi), slot="cs2", waits=[("ve", shv)])
        cs2tok = cx.dma("gpsimd", lambda e, h=h: e.dma_start(
            out=cs2[1], in_=crowlo), slot="cs2", waits=[("ve", shv)])
        cx.used("c16", "ve")
        cx.used("c16sh", cs2tok)
        cx.used("c16hi", cs2tok)
        cx.used("c16lo", cs2tok)
        pending = [None]

        def flush_l0(cx=cx, pl=pl, ps=ps, pending=pending, h=h):
            if pending[0] is None:
                return
            pn, ti, pnorm_ve = pending[0]
            pending[0] = None
            ti4 = ti % 4
            for tj in range(ti + 1):
                half = tj % 8
                pl.op("tensor", lambda e, pn=pn, tj=tj, half=half: e.transpose(
                    out=ps["ps_tp"][:, half * 128:(half + 1) * 128],
                    in_=pn[:, tj * 128:(tj + 1) * 128],
                    identity=a4(cx, "id_bf")),
                    waits=[("ve", pnorm_ve)] +
                          (cx.use_wait("pstp") if half == 0 else []),
                    inc="te")
                if half == 7 or tj == ti:
                    t_te = pl.v("te")
                    n = half + 1
                    first = tj - half
                    pl.op("vector", lambda e, n=n, first=first, ti4=ti4:
                          e.tensor_copy(
                        out=pt3[:, first:first + n,
                                ti4 * 128:ti4 * 128 + 128],
                        in_=ps["ps_tp"][:, 0:n * 128].rearrange(
                            "p (a b) -> p a b", b=128)),
                        waits=[("te", t_te)], inc="ve")
                    cx.used("pstp", "ve")
            cx.used(("pn", ti4 % 2), "te")
            if ti4 == 3:
                ic = ti // 4
                ptb_ve = pl.v("ve")
                for tj in range(4 * (ic + 1)):
                    o = max(0, (tj - 4 * ic) * 128)
                    pl.op("tensor", lambda e, tj=tj, o=o, h=h, ic=ic: e.matmul(
                        out=ps["ps_z"][0:64, o:512],
                        lhsT=v_sb[:, tj * 2048 + h * 64:
                                  tj * 2048 + h * 64 + 64],
                        rhs=pt3[:, tj, o:512],
                        start=(tj == 0), stop=(tj == 4 * (ic + 1) - 1)),
                        waits=[("ve", ptb_ve), vtok] +
                              (cx.use_wait("psz") if tj == 0 else []), inc="te")
                av_te = pl.v("te")
                cx.used("ptbuf", "te")
                pl.op("scalar", lambda e: e.activation(
                    out=zstage, in_=ps["ps_z"][0:64, 0:512], func=AF.Copy),
                    waits=[("te", av_te)] + cx.use_wait("zst"), inc="se")
                cx.used("psz", "se")
                zv = pl.v("se")
                tok = cx.dma("gpsimd", lambda e, h=h, ic=ic: e.dma_start(
                    out=z_dram[h * 64:h * 64 + 64, ic * 512:(ic + 1) * 512],
                    in_=zstage), slot="zst", waits=[("se", zv)])
                cx.used("zst", tok)

        for ic in range(4):
            for ti4 in range(4):
                ti = 4 * ic + ti4
                ji = 128 * (ti + 1)
                pn = pns[ti4 % 2]
                par = ti % 2
                negm, sums, recipc = negms[par], sumss[par], recips[par]
                ar = augrows[ti % 2]
                artok = cx.dma("sync", lambda e, ti=ti, ji=ji, ar=ar: e.dma_start(
                    out=ar[0:2, 0:ji], in_=cs2[:, ti, 0:ji]),
                    slot=("ar", ti % 2), waits=[cs2tok])
                nb = (ji + 511) // 512
                pmax = a4(cx, "pmax0") if par == 0 else a4(cx, "pmax1")
                for jc in range(nb):
                    n = min(512, ji - jc * 512)
                    pl.op("tensor", lambda e, qh=qh, h=h, jc=jc, n=n, ti=ti:
                          e.matmul(
                        out=ps["ps_s"][:, jc * 512:jc * 512 + n],
                        lhsT=qh[:, ti * 128:(ti + 1) * 128],
                        rhs=kT[(h % 2) * 64:(h % 2) * 64 + 64,
                               (h // 2) * 2048 + jc * 512:
                               (h // 2) * 2048 + jc * 512 + n],
                        start=True, stop=False),
                        waits=[qld] + cx.use_wait(("psS", jc)), inc="te")
                    pl.op("tensor", lambda e, jc=jc, n=n, ar=ar: e.matmul(
                        out=ps["ps_s"][:, jc * 512:jc * 512 + n],
                        lhsT=ones2, rhs=ar[0:2, jc * 512:jc * 512 + n],
                        start=False, stop=True),
                        waits=[artok, cx.state["consts"]], inc="te")
                    s_te = pl.v("te")
                    if jc == nb - 1:
                        dji = ji - 128
                        pl.op("vector", lambda e, dji=dji: e.tensor_tensor(
                            out=ps["ps_s"][:, dji:dji + 128],
                            in0=ps["ps_s"][:, dji:dji + 128],
                            in1=maskt, op=ALU.add),
                            waits=[("te", s_te), cx.state["consts"]] +
                                  cx.use_wait(("negm", par)), inc="ve")
                    pl.op("vector", lambda e, jc=jc, n=n, pmax=pmax:
                          e.tensor_reduce(
                        out=pmax[:, jc:jc + 1],
                        in_=ps["ps_s"][:, jc * 512:jc * 512 + n],
                        axis=AX.X, op=ALU.max),
                        waits=[("te", s_te)] +
                              (cx.use_wait(("sums", par)) if jc == 0 else []),
                        inc="ve")
                cx.used(("ar", ti % 2), "te")
                pl.op("vector", lambda e, negm=negm, pmax=pmax, nb=nb:
                      e.tensor_reduce(
                    out=negm, in_=pmax[:, 0:nb], axis=AX.X, op=ALU.max,
                    negate=True), waits=cx.use_wait(("negm", par)), inc="ve")
                mx_ve = pl.v("ve")
                sums4 = a4(cx, "sums40") if par == 0 else a4(cx, "sums41")
                for jc in range(nb):
                    n = min(512, ji - jc * 512)
                    pl.op("scalar", lambda e, pn=pn, jc=jc, n=n, negm=negm,
                          sums4=sums4: e.activation(
                        out=pn[:, jc * 512:jc * 512 + n],
                        in_=ps["ps_s"][:, jc * 512:jc * 512 + n], func=AF.Exp,
                        bias=negm, accum_out=sums4[:, jc:jc + 1]),
                        waits=[("ve", mx_ve)] +
                              (cx.use_wait(("pn", ti4 % 2)) +
                               cx.use_wait(("sums4", par)) if jc == 0 else []),
                        inc="se")
                    cx.used(("psS", jc), "se")
                e_se = pl.v("se")
                cx.used(("negm", par), "se")
                pl.op("vector", lambda e, sums=sums, sums4=sums4, nb=nb:
                      e.tensor_reduce(out=sums, in_=sums4[:, 0:nb], axis=AX.X,
                                      op=ALU.add),
                      waits=[("se", e_se)] + cx.use_wait(("sums", par)),
                      inc="ve")
                cx.used(("sums4", par), "ve")
                pl.op("vector", lambda e, recipc=recipc, sums=sums:
                      e.reciprocal(out=recipc, in_=sums),
                      waits=cx.use_wait(("recip", par)), inc="ve")
                cx.used(("sums", par), "ve")
                pl.op("vector", lambda e, pn=pn, ji=ji, recipc=recipc:
                      e.tensor_scalar(
                    out=pn[:, 0:ji], in0=pn[:, 0:ji], scalar1=recipc,
                    scalar2=None, op0=ALU.mult), inc="ve")
                cx.used(("recip", par), "ve")
                pnorm_ve = pl.v("ve")
                flush_l0()
                pending[0] = (pn, ti, pnorm_ve)
        flush_l0()
        cx.used(("qh", h % 2), "te")
    cx.pl.fence()
    z_sb = cx.vw("A1", 0, BF16, DT * T)
    ych = cx.vw("A2", 0, F32, 8192)
    ysq = cx.vw("A2", 32768, BF16, 8192)
    wocol = (cx.vw("A3", 0, BF16, 2048), cx.vw("A3", 4096, BF16, 2048),
             cx.vw("A3", 8192, BF16, 2048))
    rbs = (cx.vw("A3", 30720, F32, 512), cx.vw("A3", 32768, F32, 512))
    ztok = None
    for et in range(DT):
        ztok = cx.dma("sync", lambda e, et=et: e.dma_start(
            out=z_sb[:, et * 2048:(et + 1) * 2048],
            in_=z_dram[et * 128:(et + 1) * 128, :]), slot="zl2")
    for tc in range(4):
        for dt_ in range(DT):
            wb = wocol[dt_ % 3]
            wld = cx.dma("sync", lambda e, wb=wb, dt_=dt_: e.dma_start(
                out=wb, in_=cx.dram["wo0_s"][dt_]), slot=("woc", dt_ % 3))
            pp = ps["ps_s"][:, (dt_ % 2) * 512:(dt_ % 2 + 1) * 512]
            for ec in range(DT):
                pl.op("tensor", lambda e, pp=pp, wb=wb, ec=ec, tc=tc: e.matmul(
                    out=pp, lhsT=wb[:, ec * 128:(ec + 1) * 128],
                    rhs=z_sb[:, ec * 2048 + tc * 512: ec * 2048 + (tc + 1) * 512],
                    start=(ec == 0), stop=(ec == DT - 1)),
                    waits=[wld, ztok] +
                          (cx.use_wait(("psO", dt_ % 2)) if ec == 0 else []),
                    inc="te")
            mm_te = pl.v("te")
            cx.used(("woc", dt_ % 3), "te")
            rb = rbs[dt_ % 2]
            rld = cx.dma("sync", lambda e, rb=rb, dt_=dt_, tc=tc: e.dma_start(
                out=rb, in_=r_in[dt_ * 128:(dt_ + 1) * 128,
                                 tc * 512:(tc + 1) * 512]), slot=("xf", dt_ % 2))
            pl.op("vector", lambda e, pp=pp, rb=rb, dt_=dt_: e.tensor_add(
                out=ych[:, dt_ * 512:(dt_ + 1) * 512], in0=pp, in1=rb),
                waits=[("te", mm_te), rld] +
                      (cx.use_wait("lnych") if dt_ == 0 else []), inc="ve")
            cx.used(("xf", dt_ % 2), "ve")
            cx.used(("psO", dt_ % 2), "ve")
        y_ve = pl.v("ve")
        emit_ln_chunk(cx, tc, r_out, ych, ysq, y_ready=[("ve", y_ve)])
        cx.used("lnych", "te")


def emit_l0a(cx, seqs=(0, 1), prep=True, rin="r{s}_in", rout="r{s}_out"):
    if prep:
        emit_l0a_prep(cx)
        cx.pl.fence()
    for s in seqs:
        emit_l0a_seq(cx, s, cx.dram[rin.format(s=s)], cx.dram[rout.format(s=s)])
        cx.pl.fence()


def build_l0a_launch():
    cx = Ctx(bass.Bass("TRN2", target_bir_lowering=False))
    cx.dram_t("L0_attn_Wq", [D, D], F32, kind="ExternalInput")
    cx.dram_t("L0_attn_Wk", [D, D], F32, kind="ExternalInput")
    cx.dram_t("L0_attn_Wv", [D, D], F32, kind="ExternalInput")
    cx.dram_t("L0_attn_Wf", [H0, D], F32, kind="ExternalInput")
    cx.dram_t("L0_attn_bf", [H0], F32, kind="ExternalInput")
    cx.dram_t("L0_attn_Wo", [D, D], F32, kind="ExternalInput")
    for s in range(NSEQ):
        cx.dram_t(f"r{s}_in", [D, T], F32, kind="ExternalInput")
        cx.dram_t(f"r{s}_out", [D, T], F32, kind="ExternalOutput")
    for nm in ("wq0_s", "wk0_s", "wv0_s", "wo0_s"):
        cx.dram_t(nm, [DT, 128, D], BF16)
    cx.dram_t("wf0_s", [16, 128, 32], F32)
    cx.dram_t("c_sp", [H0, T], F32)
    cx.dram_t("cs2_sp", [2, 16, T], BF16)
    cx.dram_t("q_sp", [DT, 128, T], BF16)
    cx.dram_t("v_sp", [TT, 128, T], BF16)
    cx.dram_t("z_sp0", [D, T], BF16)
    alloc_common(cx)
    load_consts(cx)
    emit_l0a(cx)
    return finalize(cx)



def finalize(cx, self_waits=False):
    nc = cx.nc
    waits = [(s, cx.pl.v(s)) for s in cx.pl.cnt if s.startswith("dm")]
    cx.pl.op("gpsimd", lambda e: e.engine_nop(), waits=waits)
    sems = {}
    cms = []
    names = set(cx.pl.cnt.keys())
    for _, _, ws, _ in cx.pl.steps:
        names.update(s for s, _ in ws)
    for s in sorted(names):
        cm = nc.semaphore(s)
        sems[s] = cm.__enter__()
        cms.append(cm)
    cx.pl.emit(nc, sems, self_waits=self_waits)
    for cm in reversed(cms):
        cm.__exit__(None, None, None)
    return nc


def build_l0f_launch():
    cx = Ctx(bass.Bass("TRN2", target_bir_lowering=False))
    cx.dram_t("L0_ffn_w_fc", [HID0, D], F32, kind="ExternalInput")
    cx.dram_t("L0_ffn_w_proj", [D, HID0], F32, kind="ExternalInput")
    for s in range(NSEQ):
        cx.dram_t(f"r{s}_in", [D, T], F32, kind="ExternalInput")
        cx.dram_t(f"r{s}_out", [D, T], F32, kind="ExternalOutput")
    cx.dram_t("wfcT_s", [HT0, 128, D], BF16)
    cx.dram_t("wprT_s", [DT, 128, HID0], BF16)
    cx.dram_t("h_sp", [4, HT0, 128, 512], BF16)
    alloc_common(cx)
    load_consts(cx)
    emit_l0f(cx)
    return finalize(cx)


def build_full_launch():
    cx = Ctx(bass.Bass("TRN2", target_bir_lowering=False))
    cx.dram_t("L0_attn_Wq", [D, D], F32, kind="ExternalInput")
    cx.dram_t("L0_attn_Wk", [D, D], F32, kind="ExternalInput")
    cx.dram_t("L0_attn_Wv", [D, D], F32, kind="ExternalInput")
    cx.dram_t("L0_attn_Wf", [H0, D], F32, kind="ExternalInput")
    cx.dram_t("L0_attn_bf", [H0], F32, kind="ExternalInput")
    cx.dram_t("L0_attn_Wo", [D, D], F32, kind="ExternalInput")
    cx.dram_t("L0_ffn_w_fc", [HID0, D], F32, kind="ExternalInput")
    cx.dram_t("L0_ffn_w_proj", [D, HID0], F32, kind="ExternalInput")
    cx.dram_t("L1_attn_W_Q", [H1, D, E1], F32, kind="ExternalInput")
    cx.dram_t("L1_attn_W_K", [H1, D, E1], F32, kind="ExternalInput")
    cx.dram_t("L1_attn_W_V", [H1, D, E1], F32, kind="ExternalInput")
    cx.dram_t("L1_attn_W_O", [H1, E1, D], F32, kind="ExternalInput")
    cx.dram_t("L1_attn_b_Q", [H1, E1], F32, kind="ExternalInput")
    cx.dram_t("L1_attn_b_K", [H1, E1], F32, kind="ExternalInput")
    cx.dram_t("L1_attn_b_V", [H1, E1], F32, kind="ExternalInput")
    cx.dram_t("L1_attn_b_O", [D], F32, kind="ExternalInput")
    cx.dram_t("L1_ffn_w_gate", [D, HID1], I8, kind="ExternalInput")
    cx.dram_t("L1_ffn_w_up", [D, HID1], I8, kind="ExternalInput")
    cx.dram_t("L1_ffn_w_down", [HID1, D], I8, kind="ExternalInput")
    cx.dram_t("L1_ffn_s_gate", [HID1], F32, kind="ExternalInput")
    cx.dram_t("L1_ffn_s_up", [HID1], F32, kind="ExternalInput")
    cx.dram_t("L1_ffn_s_down", [D], F32, kind="ExternalInput")
    for s in range(NSEQ):
        cx.dram_t(f"r{s}_in", [D, T], F32, kind="ExternalInput")
        cx.dram_t(f"r{s}_out", [D, T], F32, kind="ExternalOutput")
        cx.dram_t(f"r{s}_a", [D, T], F32)
        cx.dram_t(f"r{s}_b", [D, T], F32)
        cx.dram_t(f"r{s}_c", [D, T], F32)
    for nm in ("wq0_s", "wk0_s", "wv0_s", "wo0_s"):
        cx.dram_t(nm, [DT, 128, D], BF16)
    cx.dram_t("wf0_s", [16, 128, 32], F32)
    cx.dram_t("c_sp", [H0, T], F32)
    cx.dram_t("cs2_sp", [2, 16, T], BF16)
    cx.dram_t("q_sp", [DT, 128, T], BF16)
    cx.dram_t("v_sp", [TT, 128, T], BF16)
    cx.dram_t("z_sp0", [D, T], BF16)
    cx.dram_t("z_sp", [H1, 128, T], BF16)
    cx.dram_t("wfcT_s", [HT0, 128, D], BF16)
    cx.dram_t("wprT_s", [DT, 128, HID0], BF16)
    cx.dram_t("h_sp", [4, HT0, 128, 512], BF16)
    cx.dram_t("wq1_s", [H1, D, E1], BF16)
    cx.dram_t("wk1_s", [H1, D, E1], BF16)
    cx.dram_t("wv1_s", [H1, D, E1], BF16)
    cx.dram_t("wo1_s", [H1, E1, D], BF16)
    cx.dram_t("wguT_s", [HT1, 128, 2 * D], BF16)
    cx.dram_t("wdT_s", [DT, 128, HID1], BF16)
    cx.dram_t("scales_s", [128, 256], F32)
    cx.dram_t("fuse_sp", [4, HT1, 128, 512], BF16)
    alloc_common(cx)
    load_consts(cx)
    emit_l0a(cx, rin="r{s}_in", rout="r{s}_a")
    emit_l0f(cx, rin="r{s}_a", rout="r{s}_b")
    emit_l1a(cx, rin="r{s}_b", rout="r{s}_c")
    emit_l1f(cx, rin="r{s}_c", rout="r{s}_out")
    return finalize(cx)


_BUILD_CACHE = {}


def _get_nc(name, builder):
    if name not in _BUILD_CACHE:
        _BUILD_CACHE[name] = builder()
    return _BUILD_CACHE[name]


def kernel(**inputs):
    ii = {k: np.asarray(v) for k, v in inputs.items()}
    nc = _get_nc("full", build_full_launch)
    wkeys = [k for k in ii if k != "L0_attn_x"]
    in_maps = []
    for c in range(NCORES):
        m = {k: ii[k] for k in wkeys}
        m["r0_in"] = np.ascontiguousarray(ii["L0_attn_x"][2 * c].T)
        m["r1_in"] = np.ascontiguousarray(ii["L0_attn_x"][2 * c + 1].T)
        in_maps.append(m)
    res = run_bass_kernel_spmd(nc, in_maps, core_ids=list(range(NCORES)))
    out = np.empty((16, T, D), dtype=np.float32)
    for c in range(NCORES):
        for s in range(2):
            out[2 * c + s] = np.asarray(res.results[c][f"r{s}_out"]).T
    return out


# version 14
# speedup vs baseline: 1.0682x; 1.0682x over previous
import numpy as np
import ml_dtypes
import concourse.bass as bass
import concourse.mybir as mybir
from concourse.bass_utils import run_bass_kernel_spmd
from concourse.ap import AP

F32 = mybir.dt.float32
BF16 = mybir.dt.bfloat16
I8 = mybir.dt.int8
AF = mybir.ActivationFunctionType
ALU = mybir.AluOpType
AX = mybir.AxisListType

D = 2048
T = 2048
NSEQ = 2
NCORES = 8
DT = D // 128
TT = T // 128
H0, E0 = 32, 64
H1, E1 = 16, 128
HID0 = 5120
HID1 = 8192
HT0 = HID0 // 128
HT1 = HID1 // 128
ESIZE = {F32: 4, BF16: 2, I8: 1}

ENG_SEMS = ("te", "ve", "se", "ge")


class Plan:
    def __init__(self):
        self.steps = []
        self.cnt = {}
        self.slotnames = {}
        self.pending = {}

    def v(self, sem):
        return self.cnt.get(sem, 0)

    def slot_sem(self, key):
        if key not in self.slotnames:
            self.slotnames[key] = f"dm{len(self.slotnames)}"
        return self.slotnames[key]

    OWN = {"tensor": "te", "vector": "ve", "scalar": "se", "gpsimd": "ge"}

    def op(self, eng, fn, waits=(), inc=None, dma=None):
        incs = []
        if dma is not None:
            self.cnt[dma] = self.cnt.get(dma, 0) + 16
            incs.append((dma, 16))
        if inc is None and dma is None and eng in self.OWN:
            inc = self.OWN[eng]
        if inc is not None:
            self.cnt[inc] = self.cnt.get(inc, 0) + 1
            incs.append((inc, 1))
        ws = [(s, v) for (s, v) in waits if v and v > 0]
        pend = self.pending.pop(eng, None)
        if pend:
            ws = pend + ws
        self.steps.append((eng, fn, ws, incs))

    def fence(self):
        snap = [(s, v) for s, v in self.cnt.items() if v > 0]
        for eng in ("tensor", "vector", "scalar", "gpsimd", "sync"):
            own = self.OWN.get(eng)
            self.pending[eng] = [(s, v) for s, v in snap if s != own]

    def emit(self, nc, sems, self_waits=False):
        with nc.Block() as blk:
            for eng_name, reg in (("tensor", blk.tensor), ("vector", blk.vector),
                                  ("scalar", blk.scalar), ("gpsimd", blk.gpsimd),
                                  ("sync", blk.sync)):
                def body(eng, eng_name=eng_name):
                    waited = {}
                    own = self.OWN.get(eng_name)
                    self_cnt = 0
                    for e, fn, ws, incs in self.steps:
                        if e != eng_name:
                            continue
                        for s, val in ws:
                            if waited.get(s, 0) >= val:
                                continue
                            eng.wait_ge(sems[s], val)
                            waited[s] = val
                        ins = fn(eng)
                        for s, amt in incs:
                            ins.then_inc(sems[s], amt)
                        if own is not None and any(s == own for s, _ in incs):
                            if self_cnt > 0 and (self_waits or
                                                 eng_name != "tensor"):
                                ins._wait_ge(sems[own], self_cnt)
                            self_cnt += 1
                reg(body)


class Ctx:
    def __init__(self, nc):
        self.nc = nc
        self.dram = {}
        self.sbuf = {}
        self.psum = {}
        self.pl = Plan()
        self.state = {}

    def dram_t(self, name, shape, dtype, kind="Internal"):
        h = self.nc.dram_tensor(name, list(shape), dtype, kind=kind)
        self.dram[name] = h.ap()
        return self.dram[name]

    def inline(self, name, arr):
        ap = self.nc.inline_tensor(np.ascontiguousarray(arr), name).ap()
        self.dram[name] = ap
        return ap

    def vw(self, arena, off_bytes, dtype, cols, parts=128, base=0):
        es = ESIZE[dtype]
        assert off_bytes % max(es, 4) == 0
        h = self.sbuf[arena] if dtype == BF16 else self.sbuf[arena].bitcast(dtype)
        o = off_bytes // es
        return h[base:base + parts, o:o + cols]

    def vwp(self, arena, off_bytes, dtype, parts, cols):
        return self.vw(arena, off_bytes, dtype, cols, parts)

    def used(self, key, sem):
        if sem is None:
            return
        d = self.state.setdefault(("use", key), {})
        if isinstance(sem, tuple):
            d[sem[0]] = max(d.get(sem[0], 0), sem[1])
        else:
            d[sem] = self.pl.v(sem)

    def use_wait(self, key):
        d = self.state.get(("use", key))
        return list(d.items()) if d else []

    def dma(self, eng, fn, slot, waits=()):
        sem = self.pl.slot_sem(slot)
        self.pl.op(eng, fn, waits=tuple(waits) + tuple(self.use_wait(slot)),
                   dma=sem)
        return (sem, self.pl.v(sem))


def dram_bcast(ap_row, parts):
    return AP(ap_row.tensor, ap_row.offset,
              [[0, parts]] + [list(x) for x in ap_row.ap[1:]])


def alloc_common(cx):
    nc = cx.nc
    for nm, cols in (("A1", 32768), ("A2", 32768), ("A3", 24576), ("A4", 6144)):
        cx.sbuf[nm] = nc.alloc_sbuf_tensor(nm, [128, cols], BF16)
    cx.psum["ps_s"] = nc.alloc_psum_tensor("ps_s", [128, 2048], F32)
    cx.psum["ps_tp"] = nc.alloc_psum_tensor("ps_tp", [128, 2048], BF16)
    cx.psum["ps_z"] = nc.alloc_psum_tensor("ps_z", [128, 512], F32)
    cx.psum["ps_x"] = nc.alloc_psum_tensor("ps_x", [128, 512], F32)
    cx.inline("c_id_bf", np.eye(128, dtype=ml_dtypes.bfloat16))
    cx.inline("c_ones128_f", np.ones((128, 1), dtype=np.float32))
    cx.inline("c_ones1_f", np.ones((1, 128), dtype=np.float32))
    cx.inline("c_onesb", np.ones((128, 1), dtype=ml_dtypes.bfloat16))
    cx.inline("c_id_f32", np.eye(128, dtype=np.float32))
    mk = np.triu(np.full((128, 128), -1e9, dtype=np.float32), 1)
    cx.inline("c_maskt", mk)
    cx.inline("c_onesrow_bf", np.ones((1, 128), dtype=ml_dtypes.bfloat16))
    cx.inline("c_ones2_bf", np.ones((2, 128), dtype=ml_dtypes.bfloat16))
    cx.dram_t("scal_bounce", [1, 16], F32)


def a4(cx, name):
    m = {
        "id_bf": ("A4", 0, BF16, 128, 128),
        "onesb": ("A4", 256, BF16, 1, 128),
        "ones128_f": ("A4", 512, F32, 1, 128),
        "ones1_f": ("A4", 1024, F32, 128, 1),
        "amc": ("A4", 1536, F32, 1, 128),
        "thrc": ("A4", 2048, F32, 1, 128),
        "nthrc": ("A4", 2560, F32, 1, 128),
        "mrow": ("A4", 3072, F32, 512, 1),
        "vrow": ("A4", 5120, F32, 512, 1),
        "rstdrow": ("A4", 7168, F32, 512, 1),
        "part": ("A4", 9216, F32, 48, 128),
        "id_f32": ("A4", 9472, F32, 128, 128),
        "sc_all": ("A4", 10048, F32, 144, 128),
        "gcols": ("A4", 10624, F32, 16, 128),
        "negm": ("A4", 10688, F32, 1, 128),
        "negm1": ("A4", 11840, F32, 1, 128),
        "sums1": ("A4", 11872, F32, 1, 128),
        "recipc1": ("A4", 11904, F32, 1, 128),
        "pmax0": ("A4", 11920, F32, 4, 128),
        "pmax1": ("A4", 11936, F32, 4, 128),
        "ngc": ("A4", 10752, F32, 1, 128),
        "bqc": ("A4", 10816, F32, 16, 128),
        "bkc": ("A4", 10880, F32, 16, 128),
        "boc": ("A4", 10944, F32, 16, 128),
        "sums": ("A4", 11008, F32, 1, 128),
        "recipc": ("A4", 11072, F32, 1, 128),
        "crow16": ("A4", 11136, F32, 1, 16),
        "maskt": ("A4", 11264, F32, 128, 128),
        "bfc": ("A4", 11776, F32, 1, 32),
        "onesrow_bf": ("A4", 11968, BF16, 128, 1),
        "ones2_bf": ("A4", 11968, BF16, 128, 2),
        "ccol16": ("A4", 12224, F32, 1, 16),
    }
    ar, off, dt_, cols, parts = m[name]
    return cx.vw(ar, off, dt_, cols, parts)


def load_consts(cx):
    tok = None
    for nm in ("id_bf", "onesb", "ones128_f", "ones1_f", "id_f32", "maskt",
               "ones2_bf"):
        srcap = cx.dram["c_" + nm]
        dst = a4(cx, nm)
        tok = cx.dma("sync", lambda e, d=dst, s=srcap: e.dma_start(out=d, in_=s),
                     slot="cst")
    cx.state["consts"] = tok



def emit_prep_transpose(cx, w_ap, rows, cols, scr_ap, ternarize, scale=1.0):
    pl, ps = cx.pl, cx.psum
    nrt, nct = rows // 128, cols // 128
    pw = (cx.vw("A3", 0, F32, cols), cx.vw("A3", 20480, F32, cols))
    gt = cx.vw("A1", 0, F32, cols)
    wtb = cx.vw("A1", 20480, BF16, cols)
    ev0 = cx.vw("A1", 31744, BF16, 2048)
    id_bf = a4(cx, "id_bf")
    thr_ve = 0
    if ternarize:
        part = a4(cx, "part")
        for rt in range(nrt):
            buf = pw[rt % 2]
            ld = cx.dma("sync", lambda e, b=buf, rt=rt: e.dma_start(
                out=b, in_=w_ap[rt * 128:(rt + 1) * 128, :]), slot=("pw", rt % 2))
            pl.op("vector", lambda e, b=buf, rt=rt: e.tensor_reduce(
                out=part[:, rt:rt + 1], in_=b, axis=AX.X, op=ALU.add,
                apply_absolute_value=True), waits=[ld], inc="ve")
            cx.used(("pw", rt % 2), "ve")
        red_ve = pl.v("ve")
        pl.op("tensor", lambda e, nrt=nrt: e.matmul(
            out=ps["ps_x"][0:1, 0:nrt], lhsT=a4(cx, "ones128_f"),
            rhs=part[:, 0:nrt], start=True, stop=True),
            waits=[("ve", red_ve), cx.state["consts"]] + cx.use_wait("ps_x"),
            inc="te")
        mm_te = pl.v("te")
        mrow = a4(cx, "mrow")
        pl.op("vector", lambda e, nrt=nrt: e.tensor_reduce(
            out=mrow[0:1, 0:1], in_=ps["ps_x"][0:1, 0:nrt], axis=AX.X, op=ALU.add),
            waits=[("te", mm_te)], inc="ve")
        cx.used("ps_x", "ve")
        pl.op("vector", lambda e, n=rows * cols: e.tensor_scalar(
            out=mrow[0:1, 0:1], in0=mrow[0:1, 0:1], scalar1=1.0 / n,
            scalar2=1e-5, op0=ALU.mult, op1=ALU.max), inc="ve")
        am_ve = pl.v("ve")
        amd = cx.dram["scal_bounce"]
        st = cx.dma("gpsimd", lambda e: e.dma_start(
            out=amd[0:1, 0:1], in_=mrow[0:1, 0:1]), slot="amd",
            waits=[("ve", am_ve)])
        amc = a4(cx, "amc")
        am_ld = cx.dma("gpsimd", lambda e, amc=amc: e.dma_start(
            out=amc, in_=dram_bcast(amd[0:1, 0:1], 128)), slot="amc",
            waits=[st])
        thrc, nthrc = a4(cx, "thrc"), a4(cx, "nthrc")
        pl.op("vector", lambda e, amc=amc, thrc=thrc: e.tensor_scalar(
            out=thrc, in0=amc, scalar1=0.7, scalar2=None, op0=ALU.mult),
            waits=[am_ld], inc="ve")
        pl.op("vector", lambda e, thrc=thrc, nthrc=nthrc: e.tensor_scalar(
            out=nthrc, in0=thrc, scalar1=-1.0, scalar2=None, op0=ALU.mult),
            inc="ve")
        cx.used("amc", "ve")
        thr_ve = pl.v("ve")

    for rt in range(nrt):
        buf = pw[rt % 2]
        ld = cx.dma("sync", lambda e, b=buf, rt=rt: e.dma_start(
            out=b, in_=w_ap[rt * 128:(rt + 1) * 128, :]), slot=("pw", rt % 2))
        if ternarize:
            amc, thrc, nthrc = a4(cx, "amc"), a4(cx, "thrc"), a4(cx, "nthrc")
            pl.op("vector", lambda e, b=buf, gt=gt: e.tensor_scalar(
                out=gt, in0=b, scalar1=thrc, scalar2=amc,
                op0=ALU.is_gt, op1=ALU.mult),
                waits=[ld, ("ve", thr_ve)] + cx.use_wait("gt"), inc="ve")
            pl.op("vector", lambda e, b=buf: e.tensor_scalar(
                out=wtb, in0=b, scalar1=nthrc, scalar2=amc,
                op0=ALU.is_lt, op1=ALU.mult), waits=cx.use_wait("wtb"), inc="ve")
            cx.used(("pw", rt % 2), "ve")
            pl.op("vector", lambda e: e.tensor_sub(out=wtb, in0=gt, in1=wtb),
                  inc="ve")
            cx.used("gt", "ve")
        elif scale == 1.0:
            pl.op("vector", lambda e, b=buf: e.tensor_copy(out=wtb, in_=b),
                  waits=[ld] + cx.use_wait("wtb"), inc="ve")
            cx.used(("pw", rt % 2), "ve")
        else:
            pl.op("vector", lambda e, b=buf, sc=scale: e.tensor_scalar(
                out=wtb, in0=b, scalar1=sc, scalar2=None, op0=ALU.mult),
                waits=[ld] + cx.use_wait("wtb"), inc="ve")
            cx.used(("pw", rt % 2), "ve")
        cast_ve = pl.v("ve")
        for g0 in range(0, nct, 16):
            gn = min(16, nct - g0)
            for j in range(gn):
                ct = g0 + j
                pl.op("tensor", lambda e, ct=ct, j=j: e.transpose(
                    out=ps["ps_tp"][:, j * 128:(j + 1) * 128],
                    in_=wtb[:, ct * 128:(ct + 1) * 128], identity=id_bf),
                    waits=[("ve", cast_ve), cx.state["consts"]] +
                          (cx.use_wait("ps_tp") if j == 0 else []), inc="te")
            tr_te = pl.v("te")
            pl.op("scalar", lambda e, gn=gn: e.activation(
                out=ev0[:, 0:gn * 128], in_=ps["ps_tp"][:, 0:gn * 128],
                func=AF.Copy), waits=[("te", tr_te)] + cx.use_wait("ev0"),
                inc="se")
            cx.used("ps_tp", "se")
            ev_se = pl.v("se")
            tok = cx.dma("gpsimd", lambda e, rt=rt, g0=g0, gn=gn: e.dma_start(
                out=scr_ap[rt, :, g0 * 128:(g0 + gn) * 128],
                in_=ev0[:, 0:gn * 128]), slot="ev0", waits=[("se", ev_se)])
            cx.used("ev0", tok)
        cx.used("wtb", "te")


def emit_prep_cast(cx, w_ap, rows, cols, scr_ap, dtype_in=F32):
    pl = cx.pl
    nrt = rows // 128
    assert cols <= 8192
    for rt in range(nrt):
        if dtype_in == I8:
            buf = cx.vw("A3", 0, I8, cols)
        else:
            buf = cx.vw("A3", 0, F32, cols)
        wtb = cx.vw("A1", 0, BF16, cols)
        ld = cx.dma("sync", lambda e, b=buf, rt=rt: e.dma_start(
            out=b, in_=w_ap[rt * 128:(rt + 1) * 128, :]), slot="pwc")
        pl.op("vector", lambda e, b=buf, w=wtb: e.tensor_copy(out=w, in_=b),
              waits=[ld] + cx.use_wait("wtbc"), inc="ve")
        cx.used("pwc", "ve")
        wv = pl.v("ve")
        tok = cx.dma("gpsimd", lambda e, rt=rt, w=wtb: e.dma_start(
            out=scr_ap[rt, :, 0:w.shape[1]], in_=w), slot="wtbc",
            waits=[("ve", wv)])
        cx.used("wtbc", tok)



def emit_ln_chunk(cx, tc, r_out, ych, ysq, y_ready=()):
    pl, ps = cx.pl, cx.psum
    ybf = cx.vw("A3", 47104, BF16, 1024)
    for dt_ in range(DT):
        pl.op("scalar", lambda e, dt_=dt_: e.activation(
            out=ysq[:, dt_ * 512:(dt_ + 1) * 512],
            in_=ych[:, dt_ * 512:(dt_ + 1) * 512], func=AF.Square),
            waits=list(y_ready) + (cx.use_wait("lnysq") if dt_ == 0 else []),
            inc="se")
    casts_se = pl.v("se")
    ones_f = a4(cx, "ones128_f")
    onesb = a4(cx, "onesb")
    for dt_ in range(DT):
        sl = dt_ % 2
        pl.op("vector", lambda e, dt_=dt_, sl=sl: e.tensor_copy(
            out=ybf[:, sl * 512:(sl + 1) * 512],
            in_=ych[:, dt_ * 512:(dt_ + 1) * 512]),
            waits=list(y_ready) + cx.use_wait(("lnybf", sl)), inc="ve")
        yb_ve = pl.v("ve")
        pl.op("tensor", lambda e, dt_=dt_, sl=sl: e.matmul(
            out=ps["ps_z"][0:1, :], lhsT=onesb,
            rhs=ybf[:, sl * 512:(sl + 1) * 512],
            start=(dt_ == 0), stop=(dt_ == DT - 1)),
            waits=[("ve", yb_ve)] + (cx.use_wait("ps_z") if dt_ == 0 else []),
            inc="te")
        cx.used(("lnybf", sl), "te")
    for dt_ in range(DT):
        pl.op("tensor", lambda e, dt_=dt_: e.matmul(
            out=ps["ps_x"][0:1, :], lhsT=onesb,
            rhs=ysq[:, dt_ * 512:(dt_ + 1) * 512],
            start=(dt_ == 0), stop=(dt_ == DT - 1)),
            waits=[("se", casts_se)] + (cx.use_wait("ps_x") if dt_ == 0 else []),
            inc="te")
    stats_te = pl.v("te")
    mrow, vrow, rstd = a4(cx, "mrow"), a4(cx, "vrow"), a4(cx, "rstdrow")
    pl.op("vector", lambda e: e.tensor_scalar(
        out=mrow, in0=ps["ps_z"][0:1, :], scalar1=1.0 / D, scalar2=None,
        op0=ALU.mult), waits=[("te", stats_te)], inc="ve")
    pl.op("vector", lambda e: e.tensor_scalar(
        out=vrow, in0=ps["ps_x"][0:1, :], scalar1=1.0 / D, scalar2=None,
        op0=ALU.mult), inc="ve")
    tnrow = cx.vw("A3", 22528, F32, 512, 1)
    pl.op("vector", lambda e, tnrow=tnrow: e.tensor_mul(
        out=tnrow, in0=mrow, in1=mrow), inc="ve")
    pl.op("vector", lambda e, tnrow=tnrow: e.tensor_sub(
        out=vrow, in0=vrow, in1=tnrow), inc="ve")
    pl.op("vector", lambda e: e.tensor_scalar(
        out=vrow, in0=vrow, scalar1=1e-5, scalar2=None, op0=ALU.add), inc="ve")
    rows_ve = pl.v("ve")
    pl.op("scalar", lambda e, tnrow=tnrow: e.activation(
        out=tnrow, in_=vrow, func=AF.Sqrt), waits=[("ve", rows_ve)], inc="se")
    sq_se = pl.v("se")
    pl.op("vector", lambda e, tnrow=tnrow: e.reciprocal(out=rstd, in_=tnrow),
          waits=[("se", sq_se)], inc="ve")
    rstd_ve = pl.v("ve")
    ones1 = a4(cx, "ones1_f")
    pl.op("tensor", lambda e: e.matmul(out=ps["ps_z"][:, :], lhsT=ones1,
                                       rhs=mrow, start=True, stop=True),
          waits=[("ve", rstd_ve)], inc="te")
    pl.op("tensor", lambda e: e.matmul(out=ps["ps_x"][:, :], lhsT=ones1,
                                       rhs=rstd, start=True, stop=True), inc="te")
    bc_te = pl.v("te")
    for dt_ in range(DT):
        sl = dt_ % 2
        rn = cx.vw("A3", 26624 + sl * 2048, F32, 512)
        tn = cx.vw("A3", 24576, F32, 512)
        pl.op("vector", lambda e, dt_=dt_, tn=tn: e.tensor_sub(
            out=tn, in0=ych[:, dt_ * 512:(dt_ + 1) * 512], in1=ps["ps_z"][:, :]),
            waits=[("te", bc_te)], inc="ve")
        pl.op("vector", lambda e, rn=rn, tn=tn: e.tensor_mul(
            out=rn, in0=tn, in1=ps["ps_x"][:, :]),
            waits=cx.use_wait(("lnrn", sl)), inc="ve")
        w_ve = pl.v("ve")
        tok = cx.dma("gpsimd", lambda e, dt_=dt_, rn=rn: e.dma_start(
            out=r_out[dt_ * 128:(dt_ + 1) * 128, tc * 512:(tc + 1) * 512], in_=rn),
            slot=("lnrn", sl), waits=[("ve", w_ve)])
        cx.used(("lnrn", sl), tok)
    cx.used("ps_z", "ve")
    cx.used("ps_x", "ve")
    cx.used("lnysq", "te")



def emit_l0f_seq(cx, s, r_in, r_out):
    pl, ps = cx.pl, cx.psum
    wfcT = cx.dram["wfcT_s"]
    wprT = cx.dram["wprT_s"]
    hsp = cx.dram["h_sp"]
    xb = cx.vw("A1", 0, BF16, DT * T)
    ych = cx.vw("A1", 16384, F32, 8192)
    ysq = cx.vw("A3", 30720, BF16, 8192)
    wcs = (cx.vw("A3", 0, BF16, 2048), cx.vw("A3", 4096, BF16, 2048))
    xfs = (cx.vw("A3", 16384, F32, 2048), cx.vw("A3", 24576, F32, 2048))
    hstg = (cx.vw("A3", 32768, BF16, 2048), cx.vw("A3", 36864, BF16, 2048))
    evr = cx.vw("A3", 40960, BF16, 512)
    wdq = (cx.vw("A3", 12288, BF16, 1280), cx.vw("A3", 14848, BF16, 1280),
           cx.vw("A3", 17408, BF16, 1280), cx.vw("A3", 19968, BF16, 1280))
    for dt_ in range(DT):
        buf = xfs[dt_ % 2]
        ld = cx.dma("sync", lambda e, b=buf, dt_=dt_: e.dma_start(
            out=b, in_=r_in[dt_ * 128:(dt_ + 1) * 128, :]), slot=("xf", dt_ % 2))
        pl.op("vector", lambda e, b=buf, dt_=dt_: e.tensor_copy(
            out=xb[:, dt_ * 2048:(dt_ + 1) * 2048], in_=b),
            waits=[ld] + (cx.use_wait("xbf") if dt_ == 0 else []), inc="ve")
        cx.used(("xf", dt_ % 2), "ve")
    xb_ve = pl.v("ve")
    for ht in range(HT0):
        wb = wcs[ht % 2]
        wld = cx.dma("sync", lambda e, wb=wb, ht=ht: e.dma_start(
            out=wb, in_=wfcT[ht, :, :]), slot=("wcf", ht % 2))
        fb = hstg[ht % 2]
        for tc in range(4):
            slot = tc % 2
            pp = ps["ps_s"][:, slot * 512:(slot + 1) * 512]
            for db in range(DT):
                pl.op("tensor", lambda e, pp=pp, wb=wb, db=db, tc=tc: e.matmul(
                    out=pp, lhsT=wb[:, db * 128:(db + 1) * 128],
                    rhs=xb[:, db * 2048 + tc * 512: db * 2048 + (tc + 1) * 512],
                    start=(db == 0), stop=(db == DT - 1)),
                    waits=[wld, ("ve", xb_ve)] +
                          (cx.use_wait(("pss", slot)) if db == 0 else []),
                    inc="te")
            mm_te = pl.v("te")
            pl.op("scalar", lambda e, pp=pp, evr=evr: e.activation(
                out=evr, in_=pp, func=AF.Relu),
                waits=[("te", mm_te)] + cx.use_wait("evr"), inc="se")
            r_se = pl.v("se")
            pl.op("scalar", lambda e, tc=tc, evr=evr, fb=fb: e.activation(
                out=fb[:, tc * 512:(tc + 1) * 512], in_=evr, func=AF.Square),
                waits=[("se", r_se)] +
                      (cx.use_wait(("hstg", ht % 2)) if tc == 0 else []),
                inc="se")
            cx.used(("pss", slot), "se")
            cx.used("evr", "se")
        h_se = pl.v("se")
        for tc in range(4):
            tok = cx.dma("gpsimd", lambda e, ht=ht, tc=tc, fb=fb: e.dma_start(
                out=hsp[tc, ht], in_=fb[:, tc * 512:(tc + 1) * 512]),
                slot=("hstg", ht % 2), waits=[("se", h_se)])
        cx.used(("hstg", ht % 2), tok)
        cx.used(("wcf", ht % 2), "te")
    cx.used("xbf", "te")
    cx.pl.fence()
    hq = cx.vw("A2", 0, BF16, HT0 * 512)
    for q in range(4):
        t0 = q * 512
        fld = cx.dma("sync", lambda e, q=q: e.dma_start(
            out=hq.rearrange("p (a b) -> p a b", b=512),
            in_=hsp[q].rearrange("ht p b -> p ht b")), slot="hq")
        for dt_ in range(DT):
            slot = dt_ % 2
            toks = {}
            for g in range(4):
                gi = dt_ * 4 + g
                b = wdq[gi % 4]
                toks[g] = cx.dma("sync", lambda e, b=b, dt_=dt_, g=g: e.dma_start(
                    out=b, in_=wprT[dt_, :, g * 1280:(g + 1) * 1280]),
                    slot=("wdq", gi % 4))
            pp = ps["ps_s"][:, (2 + slot) * 512:(3 + slot) * 512]
            for hb in range(HT0):
                g = hb // 10
                gi = dt_ * 4 + g
                wb = wdq[gi % 4]
                co = (hb % 10) * 128
                pl.op("tensor", lambda e, pp=pp, wb=wb, co=co, hb=hb: e.matmul(
                    out=pp, lhsT=wb[:, co:co + 128],
                    rhs=hq[:, hb * 512:(hb + 1) * 512],
                    start=(hb == 0), stop=(hb == HT0 - 1)),
                    waits=[toks[g], fld] +
                          (cx.use_wait(("psp", slot)) if hb == 0 else []),
                    inc="te")
                if hb % 10 == 9:
                    cx.used(("wdq", gi % 4), "te")
            mm_te = pl.v("te")
            rb = cx.vw("A3", 43008 + (dt_ % 2) * 2048, F32, 512)
            rld = cx.dma("sync", lambda e, rb=rb, dt_=dt_, t0=t0: e.dma_start(
                out=rb, in_=r_in[dt_ * 128:(dt_ + 1) * 128, t0:t0 + 512]),
                slot=("rr", dt_ % 2))
            pl.op("vector", lambda e, pp=pp, rb=rb, dt_=dt_: e.tensor_add(
                out=ych[:, dt_ * 512:(dt_ + 1) * 512], in0=pp, in1=rb),
                waits=[("te", mm_te), rld] +
                      (cx.use_wait("lnych") if dt_ == 0 else []), inc="ve")
            cx.used(("rr", dt_ % 2), "ve")
            cx.used(("psp", slot), "ve")
        y_ve = pl.v("ve")
        emit_ln_chunk(cx, q, r_out, ych, ysq, y_ready=[("ve", y_ve)])
        cx.used("lnych", "te")
        cx.used("hq", "te")
    cx.state[("use", "hq")] = cx.state.get(("use", "hq"), {})


def emit_l0f(cx, seqs=(0, 1), prep=True, rin="r{s}_in", rout="r{s}_out"):
    if prep:
        emit_prep_transpose(cx, cx.dram["L0_ffn_w_fc"], HID0, D,
                            cx.dram["wfcT_s"], ternarize=True)
        cx.pl.fence()
        emit_prep_transpose(cx, cx.dram["L0_ffn_w_proj"], D, HID0,
                            cx.dram["wprT_s"], ternarize=True)
        cx.pl.fence()
    for s in seqs:
        emit_l0f_seq(cx, s, cx.dram[rin.format(s=s)], cx.dram[rout.format(s=s)])
        cx.pl.fence()



def emit_l1f_prep(cx):
    pl, ps = cx.pl, cx.psum
    for wname, scr_name, nr, nc_, nblk in (
            ("L1_ffn_w_gate", "wguT_s", DT, HID1, HT1),
            ("L1_ffn_w_up", "wguT_s", DT, HID1, HT1),
            ("L1_ffn_w_down", "wdT_s", HT1, D, DT)):
        w_ap = cx.dram[wname]
        scr = cx.dram[scr_name]
        coff = 2048 if wname == "L1_ffn_w_up" else 0
        buf8 = cx.vw("A3", 0, I8, nc_)
        wtb = cx.vw("A1", 0, BF16, nc_)
        for rt in range(nr):
            ld = cx.dma("sync", lambda e, b=buf8, rt=rt, wa=w_ap: e.dma_start(
                out=b, in_=wa[rt * 128:(rt + 1) * 128, :]), slot="pwc")
            pl.op("vector", lambda e, b=buf8, w=wtb: e.tensor_copy(out=w, in_=b),
                  waits=[ld] + cx.use_wait("wtbc"), inc="ve")
            cx.used("pwc", "ve")
            wv = pl.v("ve")
            for g0 in range(0, nblk, 16):
                gn = min(16, nblk - g0)
                out_ap = scr[g0:g0 + gn, :,
                             coff + rt * 128:coff + (rt + 1) * 128]
                out_r = out_ap.rearrange("ct dmod hm -> dmod ct hm")
                in_r = wtb[:, g0 * 128:(g0 + gn) * 128].rearrange(
                    "dmod (ct hm) -> dmod ct hm", hm=128)
                tok = cx.dma("gpsimd", lambda e, o=out_r, i=in_r: e.dma_start(
                    out=o, in_=i), slot="wtbc", waits=[("ve", wv)])
                cx.used("wtbc", tok)
    for sname, dest_off, nblk in (("L1_ffn_s_gate", 0, HT1),
                                  ("L1_ffn_s_up", 256, HT1),
                                  ("L1_ffn_s_down", 512, DT)):
        s_ap = cx.dram[sname]
        srcv = s_ap.rearrange("(a b) -> a b", b=128)
        stmp = cx.vw("A3", 0, F32, 128, parts=nblk)
        ld = cx.dma("sync", lambda e, d=stmp, s=srcv: e.dma_start(out=d, in_=s),
                    slot="pwc")
        idf = a4(cx, "id_f32")
        pl.op("tensor", lambda e, d=stmp, idf=idf, nblk=nblk: e.transpose(
            out=ps["ps_z"][0:128, 0:nblk], in_=d,
            identity=idf[0:nblk, 0:nblk]),
            waits=[ld, cx.state["consts"]] + cx.use_wait("ps_z"), inc="te")
        tr = pl.v("te")
        dst = cx.dram["scales_s"]
        sc_sb = cx.vw("A3", 8192, F32, 128, parts=128)
        pl.op("scalar", lambda e, d=sc_sb, nblk=nblk: e.activation(
            out=d[:, 0:nblk], in_=ps["ps_z"][0:128, 0:nblk], func=AF.Copy),
            waits=[("te", tr)] + cx.use_wait("scst"), inc="se")
        cx.used("ps_z", "se")
        sv = pl.v("se")
        tok = cx.dma("gpsimd", lambda e, d=dst, o=dest_off, n=nblk, s=sc_sb:
                     e.dma_start(out=d[:, o // 4:o // 4 + n], in_=s[:, 0:n]),
                     slot="scst", waits=[("se", sv)])
        cx.used("pwc", "te")
        cx.used("scst", tok)


def emit_l1f_seq(cx, s, r_in, r_out):
    pl, ps = cx.pl, cx.psum
    wguT, wdT = cx.dram["wguT_s"], cx.dram["wdT_s"]
    fsp = cx.dram["fuse_sp"]
    xb = cx.vw("A1", 0, BF16, DT * T)
    ych = cx.vw("A1", 16384, F32, 8192)
    wdh = (cx.vw("A1", 49152, BF16, 2048), cx.vw("A1", 53248, BF16, 2048),
           cx.vw("A1", 57344, BF16, 2048), cx.vw("A1", 61440, BF16, 2048))
    ysq = cx.vw("A3", 30720, BF16, 8192)
    wgu = (cx.vw("A3", 0, BF16, 4096), cx.vw("A3", 8192, BF16, 4096))
    xfs = (cx.vw("A3", 16384, F32, 2048), cx.vw("A3", 24576, F32, 2048))
    fstg = (cx.vw("A3", 32768, BF16, 2048), cx.vw("A3", 36864, BF16, 2048))
    silb = cx.vw("A3", 40960, BF16, 512)
    sc_all = a4(cx, "sc_all")
    scld = cx.dma("sync", lambda e, d=sc_all: e.dma_start(
        out=d, in_=cx.dram["scales_s"][:, 0:144]), slot="scld")
    for dt_ in range(DT):
        buf = xfs[dt_ % 2]
        ld = cx.dma("sync", lambda e, b=buf, dt_=dt_: e.dma_start(
            out=b, in_=r_in[dt_ * 128:(dt_ + 1) * 128, :]), slot=("xf", dt_ % 2))
        pl.op("vector", lambda e, b=buf, dt_=dt_: e.tensor_copy(
            out=xb[:, dt_ * 2048:(dt_ + 1) * 2048], in_=b),
            waits=[ld] + (cx.use_wait("xbf") if dt_ == 0 else []), inc="ve")
        cx.used(("xf", dt_ % 2), "ve")
    xb_ve = pl.v("ve")
    for ht in range(HT1):
        gub = wgu[ht % 2]
        gld = cx.dma("sync", lambda e, b=gub, ht=ht: e.dma_start(
            out=b, in_=wguT[ht, :, :]), slot=("wg", ht % 2))
        fb = fstg[ht % 2]
        for tc in range(4):
            ppg = ps["ps_s"][:, 0:512]
            ppu = ps["ps_s"][:, 512:1024]
            for db in range(DT):
                pl.op("tensor", lambda e, pp=ppg, b=gub, db=db, tc=tc: e.matmul(
                    out=pp, lhsT=b[:, db * 128:(db + 1) * 128],
                    rhs=xb[:, db * 2048 + tc * 512: db * 2048 + (tc + 1) * 512],
                    start=(db == 0), stop=(db == DT - 1)),
                    waits=[gld, ("ve", xb_ve)] +
                          (cx.use_wait("psg") if db == 0 else []), inc="te")
            for db in range(DT):
                pl.op("tensor", lambda e, pp=ppu, b=gub, db=db, tc=tc: e.matmul(
                    out=pp, lhsT=b[:, 2048 + db * 128:2048 + (db + 1) * 128],
                    rhs=xb[:, db * 2048 + tc * 512: db * 2048 + (tc + 1) * 512],
                    start=(db == 0), stop=(db == DT - 1)),
                    waits=[gld, ("ve", xb_ve)] +
                          (cx.use_wait("psu") if db == 0 else []), inc="te")
            mm_te = pl.v("te")
            pl.op("scalar", lambda e, pp=ppg, ht=ht: e.activation(
                out=silb, in_=pp, func=AF.Silu,
                scale=sc_all[:, ht:ht + 1]),
                waits=[("te", mm_te), scld] + cx.use_wait("silb"), inc="se")
            cx.used("psg", "se")
            sil_se = pl.v("se")
            pl.op("vector", lambda e, pp=ppu, ht=ht, fb=fb, tc=tc:
                  e.scalar_tensor_tensor(
                out=fb[:, tc * 512:(tc + 1) * 512], in0=pp,
                scalar=sc_all[:, 64 + ht:65 + ht], in1=silb,
                op0=ALU.mult, op1=ALU.mult),
                waits=[("se", sil_se)] +
                      (cx.use_wait(("fstg", ht % 2)) if tc == 0 else []),
                inc="ve")
            cx.used("psu", "ve")
            cx.used("silb", "ve")
        f_ve = pl.v("ve")
        for tc in range(4):
            tok = cx.dma("gpsimd", lambda e, ht=ht, tc=tc, fb=fb: e.dma_start(
                out=fsp[tc, ht], in_=fb[:, tc * 512:(tc + 1) * 512]),
                slot=("fstg", ht % 2), waits=[("ve", f_ve)])
        cx.used(("fstg", ht % 2), tok)
        cx.used(("wg", ht % 2), "te")
    cx.used("xbf", "te")
    cx.pl.fence()
    fuseq = cx.vw("A2", 0, BF16, HT1 * 512)
    for q in range(4):
        t0 = q * 512
        fld = cx.dma("sync", lambda e, q=q: e.dma_start(
            out=fuseq.rearrange("p (a b) -> p a b", b=512),
            in_=fsp[q].rearrange("ht p b -> p ht b")), slot="fq")
        for dt_ in range(DT):
            toks = {}
            for g in range(4):
                gi = dt_ * 4 + g
                b = wdh[gi % 4]
                toks[g] = cx.dma("sync", lambda e, b=b, dt_=dt_, g=g: e.dma_start(
                    out=b, in_=wdT[dt_, :, g * 2048:(g + 1) * 2048]),
                    slot=("wd", gi % 4))
            pp = ps["ps_s"][:, 1024 + (dt_ % 2) * 512:1536 + (dt_ % 2) * 512]
            for hb in range(HT1):
                g = hb // 16
                gi = dt_ * 4 + g
                b = wdh[gi % 4]
                co = (hb % 16) * 128
                pl.op("tensor", lambda e, pp=pp, b=b, co=co, hb=hb: e.matmul(
                    out=pp, lhsT=b[:, co:co + 128],
                    rhs=fuseq[:, hb * 512:(hb + 1) * 512],
                    start=(hb == 0), stop=(hb == HT1 - 1)),
                    waits=[toks[g], fld] +
                          (cx.use_wait(("psd", dt_ % 2)) if hb == 0 else []),
                    inc="te")
                if hb % 16 == 15:
                    cx.used(("wd", gi % 4), "te")
            mm_te = pl.v("te")
            rb = cx.vw("A3", 16384 + (dt_ % 2) * 2048, F32, 512)
            rld = cx.dma("sync", lambda e, rb=rb, dt_=dt_, t0=t0: e.dma_start(
                out=rb, in_=r_in[dt_ * 128:(dt_ + 1) * 128, t0:t0 + 512]),
                slot=("xf", dt_ % 2))
            pl.op("vector", lambda e, pp=pp, rb=rb, dt_=dt_: e.scalar_tensor_tensor(
                out=ych[:, dt_ * 512:(dt_ + 1) * 512], in0=pp,
                scalar=sc_all[:, 128 + dt_:129 + dt_], in1=rb,
                op0=ALU.mult, op1=ALU.add),
                waits=[("te", mm_te), rld] +
                      (cx.use_wait("lnych") if dt_ == 0 else []), inc="ve")
            cx.used(("xf", dt_ % 2), "ve")
            cx.used(("psd", dt_ % 2), "ve")
        y_ve = pl.v("ve")
        emit_ln_chunk(cx, q, r_out, ych, ysq, y_ready=[("ve", y_ve)])
        cx.used("lnych", "te")
        cx.used("fq", "te")
    cx.used("scld", "ve")


def emit_l1f(cx, seqs=(0, 1), prep=True, rin="r{s}_in", rout="r{s}_out"):
    if prep:
        emit_l1f_prep(cx)
        cx.pl.fence()
    for s in seqs:
        emit_l1f_seq(cx, s, cx.dram[rin.format(s=s)], cx.dram[rout.format(s=s)])
        cx.pl.fence()


def build_l1f_launch():
    cx = Ctx(bass.Bass("TRN2", target_bir_lowering=False))
    cx.dram_t("L1_ffn_w_gate", [D, HID1], I8, kind="ExternalInput")
    cx.dram_t("L1_ffn_w_up", [D, HID1], I8, kind="ExternalInput")
    cx.dram_t("L1_ffn_w_down", [HID1, D], I8, kind="ExternalInput")
    cx.dram_t("L1_ffn_s_gate", [HID1], F32, kind="ExternalInput")
    cx.dram_t("L1_ffn_s_up", [HID1], F32, kind="ExternalInput")
    cx.dram_t("L1_ffn_s_down", [D], F32, kind="ExternalInput")
    for s in range(NSEQ):
        cx.dram_t(f"r{s}_in", [D, T], F32, kind="ExternalInput")
        cx.dram_t(f"r{s}_out", [D, T], F32, kind="ExternalOutput")
    cx.dram_t("wguT_s", [HT1, 128, 2 * D], BF16)
    cx.dram_t("wdT_s", [DT, 128, HID1], BF16)
    cx.dram_t("scales_s", [128, 256], F32)
    cx.dram_t("fuse_sp", [4, HT1, 128, 512], BF16)
    alloc_common(cx)
    load_consts(cx)
    emit_l1f(cx)
    return finalize(cx)



ATTN_SCALE1 = float(np.sqrt(np.float32(E1)))


def emit_l1a_prep(cx):
    pl, ps = cx.pl, cx.psum
    for wname, scr in (("L1_attn_W_Q", "wq1_s"), ("L1_attn_W_K", "wk1_s"),
                       ("L1_attn_W_V", "wv1_s")):
        w = cx.dram[wname].rearrange("h (dd w) e -> (h dd) (w e)", w=64)
        sc = cx.dram[scr].rearrange("h (dd w) e -> (h dd) (w e)", w=64)
        sc3 = AP(sc.tensor, sc.offset,
                 [[8192, 4], [8192 * 4 if False else 0, 0]]) if False else None
        scv = cx.dram[scr].rearrange("h (dd w) e -> (h dd) (w e)", w=64)
        scv = scv.rearrange("(a p) c -> a p c", p=128)
        wv = w.rearrange("(a p) c -> a p c", p=128)
        for rt in range(4):
            buf = cx.vw("A3", 0, F32, 8192)
            wtb = cx.vw("A1", 0, BF16, 8192)
            ld = cx.dma("sync", lambda e, b=buf, rt=rt, wv=wv: e.dma_start(
                out=b, in_=wv[rt]), slot="pwc")
            pl.op("vector", lambda e, b=buf, w_=wtb: e.tensor_copy(out=w_, in_=b),
                  waits=[ld] + cx.use_wait("wtbc"), inc="ve")
            cx.used("pwc", "ve")
            vv = pl.v("ve")
            tok = cx.dma("gpsimd", lambda e, rt=rt, w_=wtb, scv=scv: e.dma_start(
                out=scv[rt], in_=w_), slot="wtbc", waits=[("ve", vv)])
            cx.used("wtbc", tok)
    wo = cx.dram["L1_attn_W_O"].rearrange("h e d -> (h e) d")
    wov = wo.rearrange("(a p) c -> a p c", p=128)
    scov = cx.dram["wo1_s"].rearrange("h e d -> (h e) d").rearrange(
        "(a p) c -> a p c", p=128)
    for rt in range(16):
        buf = cx.vw("A3", 0, F32, 2048)
        wtb = cx.vw("A1", 0, BF16, 2048)
        ld = cx.dma("sync", lambda e, b=buf, rt=rt, wov=wov: e.dma_start(
            out=b, in_=wov[rt]), slot="pwc")
        pl.op("vector", lambda e, b=buf, w_=wtb: e.tensor_copy(out=w_, in_=b),
              waits=[ld] + cx.use_wait("wtbc"), inc="ve")
        cx.used("pwc", "ve")
        vv = pl.v("ve")
        tok = cx.dma("gpsimd", lambda e, rt=rt, w_=wtb, scov=scov: e.dma_start(
            out=scov[rt], in_=w_), slot="wtbc", waits=[("ve", vv)])
        cx.used("wtbc", tok)
    idf = a4(cx, "id_f32")
    for bname, dest, scale in (("L1_attn_b_Q", "bqc", 1.0 / ATTN_SCALE1),
                               ("L1_attn_b_K", "bkc", 1.0),
                               ("L1_attn_b_O", "boc", 1.0)):
        b_ap = cx.dram[bname]
        if bname == "L1_attn_b_O":
            b_ap = b_ap.rearrange("(a b) -> a b", b=128)
        btmp = cx.vw("A3", 0, F32, 128, parts=16)
        ld = cx.dma("sync", lambda e, d=btmp, s=b_ap: e.dma_start(out=d, in_=s),
                    slot="pwc")
        pl.op("tensor", lambda e, d=btmp, idf=idf: e.transpose(
            out=ps["ps_z"][0:128, 0:16], in_=d, identity=idf[0:16, 0:16]),
            waits=[ld, cx.state["consts"]] + cx.use_wait("ps_z"), inc="te")
        tr = pl.v("te")
        cx.used("pwc", "te")
        pl.op("scalar", lambda e, dest=dest, sc=scale: e.activation(
            out=a4(cx, dest), in_=ps["ps_z"][0:128, 0:16], func=AF.Identity,
            scale=sc), waits=[("te", tr)], inc="se")
        cx.used("ps_z", "se")


def emit_l1a_seq(cx, s, r_in, r_out):
    pl, ps = cx.pl, cx.psum
    q_dram, v_dram, z_dram = cx.dram["q_sp"], cx.dram["v_sp"], cx.dram["z_sp"]
    xb = cx.vw("A1", 0, BF16, DT * T)
    kT = cx.vw("A2", 0, BF16, DT * T)
    wcols = (cx.vw("A3", 0, BF16, 2048), cx.vw("A3", 4096, BF16, 2048))
    xfs = (cx.vw("A3", 30720, F32, 512), cx.vw("A3", 32768, F32, 512))
    qsp = cx.vw("A3", 34816, BF16, 2048)
    bvrow = cx.vw("A3", 38912, F32, 2048, parts=1)
    wvall = cx.vw("A3", 8192, BF16, 16 * 512)
    for dt_ in range(DT):
        for tc in range(4):
            buf = xfs[tc % 2]
            ld = cx.dma("sync", lambda e, b=buf, dt_=dt_, tc=tc: e.dma_start(
                out=b, in_=r_in[dt_ * 128:(dt_ + 1) * 128,
                                tc * 512:(tc + 1) * 512]), slot=("xf", tc % 2))
            pl.op("vector", lambda e, b=buf, dt_=dt_, tc=tc: e.tensor_copy(
                out=xb[:, dt_ * 2048 + tc * 512: dt_ * 2048 + (tc + 1) * 512],
                in_=b), waits=[ld], inc="ve")
            cx.used(("xf", tc % 2), "ve")
    xb_ve = pl.v("ve")
    bvld = cx.dma("sync", lambda e: e.dma_start(
        out=bvrow, in_=cx.dram["L1_attn_b_V"].rearrange("a b -> (a b)").rearrange(
            "(a c) -> a c", a=1)), slot="bv")
    for which, scr, bias, scale in (("k", "wk1_s", "bkc", None),
                                    ("q", "wq1_s", "bqc", 1.0 / ATTN_SCALE1)):
        for et in range(H1):
            wb = wcols[et % 2]
            wld = cx.dma("sync", lambda e, wb=wb, et=et, scr=scr: e.dma_start(
                out=wb.rearrange("p (db em) -> p db em", em=128),
                in_=cx.dram[scr][et].rearrange(
                    "(db dmod) em -> dmod db em", dmod=128)),
                slot=("wc", et % 2))
            for tc in range(4):
                pp = ps["ps_s"][:, (tc % 2) * 512:(tc % 2 + 1) * 512]
                for db in range(DT):
                    pl.op("tensor", lambda e, pp=pp, wb=wb, db=db, tc=tc: e.matmul(
                        out=pp, lhsT=wb[:, db * 128:(db + 1) * 128],
                        rhs=xb[:, db * 2048 + tc * 512: db * 2048 + (tc + 1) * 512],
                        start=(db == 0), stop=(db == DT - 1)),
                        waits=[wld, ("ve", xb_ve)] +
                              (cx.use_wait(("psqk", tc % 2)) if db == 0 else []),
                        inc="te")
                mm_te = pl.v("te")
                if which == "k":
                    pl.op("scalar", lambda e, pp=pp, et=et, tc=tc: e.activation(
                        out=kT[:, et * 2048 + tc * 512: et * 2048 + (tc + 1) * 512],
                        in_=pp, func=AF.Identity, bias=a4(cx, "bkc")[:, et:et + 1]),
                        waits=[("te", mm_te), cx.state["consts"]], inc="se")
                else:
                    pl.op("scalar", lambda e, pp=pp, tc=tc, et=et: e.activation(
                        out=qsp[:, tc * 512:(tc + 1) * 512], in_=pp,
                        func=AF.Identity,
                        bias=a4(cx, "bqc")[:, et:et + 1], scale=1.0 / ATTN_SCALE1),
                        waits=[("te", mm_te)] + cx.use_wait("qspb"), inc="se")
                cx.used(("psqk", tc % 2), "se")
            cx.used(("wc", et % 2), "te")
            if which == "q":
                sv = pl.v("se")
                tok = cx.dma("gpsimd", lambda e, et=et: e.dma_start(
                    out=q_dram[et], in_=qsp), slot="qspst", waits=[("se", sv)])
                cx.used("qspb", tok)
                cx.used(("qspb", 0), tok)
                cx.used(("qspb", 1), tok)
    for ec in range(4):
        wvtok = None
        for db in range(DT):
            wvtok = cx.dma("sync", lambda e, db=db, ec=ec: e.dma_start(
                out=wvall[:, db * 512:(db + 1) * 512].rearrange(
                    "p (h em) -> p h em", em=128),
                in_=cx.dram["wv1_s"][4 * ec:4 * ec + 4, db * 128:(db + 1) * 128, :]
                .rearrange("h dmod em -> dmod h em")), slot="wvall")
        for tt in range(TT):
            pp = ps["ps_s"][:, 1024 + (tt % 2) * 512:1536 + (tt % 2) * 512]
            for db in range(DT):
                pl.op("tensor", lambda e, pp=pp, db=db, tt=tt: e.matmul(
                    out=pp, lhsT=xb[:, db * 2048 + tt * 128: db * 2048 + tt * 128 + 128],
                    rhs=wvall[:, db * 512:(db + 1) * 512],
                    start=(db == 0), stop=False),
                    waits=[("ve", xb_ve), wvtok] +
                          (cx.use_wait(("psv", tt % 2)) if db == 0 else []),
                    inc="te")
            pl.op("tensor", lambda e, pp=pp, ec=ec: e.matmul(
                out=pp, lhsT=a4(cx, "ones1_f"),
                rhs=bvrow[0:1, ec * 512:(ec + 1) * 512],
                start=False, stop=True), waits=[bvld], inc="te")
            mm_te = pl.v("te")
            pl.op("scalar", lambda e, pp=pp, tt=tt: e.activation(
                out=qsp[:, (tt % 2) * 512:(tt % 2) * 512 + 512], in_=pp,
                func=AF.Copy),
                waits=[("te", mm_te)] + cx.use_wait(("qspb", tt % 2)), inc="se")
            cx.used(("psv", tt % 2), "se")
            sv = pl.v("se")
            tok = cx.dma("gpsimd", lambda e, tt=tt, ec=ec: e.dma_start(
                out=v_dram[tt, :, ec * 512:(ec + 1) * 512],
                in_=qsp[:, (tt % 2) * 512:(tt % 2) * 512 + 512]),
                slot=("qspst", tt % 2), waits=[("se", sv)])
            cx.used(("qspb", tt % 2), tok)
        cx.used("wvall", "te")
    cx.pl.fence()
    v_sb = cx.vw("A1", 0, BF16, DT * T)
    qhs = (cx.vw("A3", 0, BF16, 2048), cx.vw("A3", 4096, BF16, 2048))
    pns = (cx.vw("A3", 8192, BF16, 2048), cx.vw("A3", 12288, BF16, 2048))
    ptbuf = cx.vw("A3", 16384, BF16, 8192)
    pt3 = ptbuf.rearrange("p (a b) -> p a b", b=512)
    zstage = cx.vw("A3", 34816, BF16, 512)
    vtok = None
    for tt in range(TT):
        vtok = cx.dma("sync", lambda e, tt=tt: e.dma_start(
            out=v_sb[:, tt * 2048:(tt + 1) * 2048], in_=v_dram[tt]), slot="vld")
    negms = (a4(cx, "negm"), a4(cx, "negm1"))
    sumss = (a4(cx, "sums"), a4(cx, "sums1"))
    recips = (a4(cx, "recipc"), a4(cx, "recipc1"))
    maskt = a4(cx, "maskt")
    for h in range(H1):
        qh = qhs[h % 2]
        qld = cx.dma("sync", lambda e, h=h, qh=qh: e.dma_start(
            out=qh, in_=q_dram[h]), slot=("qh", h % 2))
        pending = [None]

        def flush_l1(cx=cx, pl=pl, ps=ps, pending=pending):
            if pending[0] is None:
                return
            pn, ti, pnorm_ve = pending[0]
            pending[0] = None
            ti4 = ti % 4
            for tj in range(ti + 1):
                half = tj % 8
                pl.op("tensor", lambda e, pn=pn, tj=tj, half=half: e.transpose(
                    out=ps["ps_tp"][:, half * 128:(half + 1) * 128],
                    in_=pn[:, tj * 128:(tj + 1) * 128],
                    identity=a4(cx, "id_bf")),
                    waits=[pnorm_ve] +
                          (cx.use_wait("pstp") if half == 0 else []),
                    inc="te")
                if half == 7 or tj == ti:
                    t_te = pl.v("te")
                    n = half + 1
                    first = tj - half
                    pl.op("vector", lambda e, n=n, first=first, ti4=ti4:
                          e.tensor_copy(
                        out=pt3[:, first:first + n,
                                ti4 * 128:ti4 * 128 + 128],
                        in_=ps["ps_tp"][:, 0:n * 128].rearrange(
                            "p (a b) -> p a b", b=128)),
                        waits=[("te", t_te)], inc="ve")
                    cx.used("pstp", "ve")
            cx.used(("pn", ti4 % 2), "te")
            if ti4 == 3:
                ic = ti // 4
                ptb_ve = pl.v("ve")
                for tj in range(4 * (ic + 1)):
                    o = max(0, (tj - 4 * ic) * 128)
                    pl.op("tensor", lambda e, tj=tj, o=o, h=h, ic=ic: e.matmul(
                        out=ps["ps_z"][:, o:512],
                        lhsT=v_sb[:, tj * 2048 + h * 128:
                                  tj * 2048 + h * 128 + 128],
                        rhs=pt3[:, tj, o:512],
                        start=(tj == 0), stop=(tj == 4 * (ic + 1) - 1)),
                        waits=[("ve", ptb_ve), vtok] +
                              (cx.use_wait("psz") if tj == 0 else []), inc="te")
                    cx.used("ptbuf", "te")
                av_te = pl.v("te")
                pl.op("scalar", lambda e: e.activation(
                    out=zstage, in_=ps["ps_z"][:, 0:512], func=AF.Copy),
                    waits=[("te", av_te)] + cx.use_wait("zst"), inc="se")
                cx.used("psz", "se")
                zv = pl.v("se")
                tok = cx.dma("gpsimd", lambda e, h=h, ic=ic: e.dma_start(
                    out=z_dram[h, :, ic * 512:(ic + 1) * 512], in_=zstage),
                    slot="zst", waits=[("se", zv)])
                cx.used("zst", tok)

        for ic in range(4):
            for ti4 in range(4):
                ti = 4 * ic + ti4
                ji = 128 * (ti + 1)
                pn = pns[ti4 % 2]
                par = ti % 2
                negm, sums, recipc = negms[par], sumss[par], recips[par]
                nb = (ji + 511) // 512
                pmax = a4(cx, "pmax0") if par == 0 else a4(cx, "pmax1")
                for jc in range(nb):
                    n = min(512, ji - jc * 512)
                    pl.op("tensor", lambda e, qh=qh, h=h, jc=jc, n=n, ti=ti:
                          e.matmul(
                        out=ps["ps_s"][:, jc * 512:jc * 512 + n],
                        lhsT=qh[:, ti * 128:(ti + 1) * 128],
                        rhs=kT[:, h * 2048 + jc * 512: h * 2048 + jc * 512 + n],
                        start=True, stop=True),
                        waits=[qld] + cx.use_wait(("psS", jc)), inc="te")
                    s_te = pl.v("te")
                    if jc == nb - 1:
                        dji = ji - 128
                        pl.op("vector", lambda e, dji=dji: e.tensor_tensor(
                            out=ps["ps_s"][:, dji:dji + 128],
                            in0=ps["ps_s"][:, dji:dji + 128],
                            in1=maskt, op=ALU.add),
                            waits=[("te", s_te), cx.state["consts"]] +
                                  cx.use_wait(("negm", par)), inc="ve")
                    pl.op("vector", lambda e, jc=jc, n=n, pmax=pmax:
                          e.tensor_reduce(
                        out=pmax[:, jc:jc + 1],
                        in_=ps["ps_s"][:, jc * 512:jc * 512 + n],
                        axis=AX.X, op=ALU.max),
                        waits=[("te", s_te)] +
                              (cx.use_wait(("sums", par)) if jc == 0 else []),
                        inc="ve")
                pl.op("vector", lambda e, negm=negm, pmax=pmax, nb=nb:
                      e.tensor_reduce(
                    out=negm, in_=pmax[:, 0:nb], axis=AX.X, op=ALU.max,
                    negate=True), waits=cx.use_wait(("negm", par)), inc="ve")
                mx_ve = pl.v("ve")
                pl.op("scalar", lambda e, pn=pn, ji=ji, negm=negm, sums=sums:
                      e.activation(
                    out=pn[:, 0:ji], in_=ps["ps_s"][:, 0:ji], func=AF.Exp,
                    bias=negm, accum_out=sums),
                    waits=[("ve", mx_ve)] + cx.use_wait(("pn", ti4 % 2)),
                    inc="se")
                e_se = pl.v("se")
                cx.used(("negm", par), "se")
                for jc in range((ji + 511) // 512):
                    cx.used(("psS", jc), "se")
                pl.op("vector", lambda e, recipc=recipc, sums=sums:
                      e.reciprocal(out=recipc, in_=sums),
                      waits=[("se", e_se)] + cx.use_wait(("recip", par)),
                      inc="ve")
                cx.used(("sums", par), "ve")
                rc_ve = pl.v("ve")
                pl.op("scalar", lambda e, pn=pn, ji=ji, recipc=recipc:
                      e.activation(
                    out=pn[:, 0:ji], in_=pn[:, 0:ji], func=AF.Identity,
                    scale=recipc), waits=[("ve", rc_ve)], inc="se")
                cx.used(("recip", par), "se")
                pnorm_ve = ("se", pl.v("se"))
                flush_l1()
                pending[0] = (pn, ti, pnorm_ve)
        flush_l1()
        cx.used(("qh", h % 2), "te")
    cx.pl.fence()
    z_sb = cx.vw("A1", 0, BF16, DT * T)
    ych = cx.vw("A2", 0, F32, 8192)
    ysq = cx.vw("A2", 32768, BF16, 8192)
    wocol = (cx.vw("A3", 0, BF16, 2048), cx.vw("A3", 4096, BF16, 2048),
             cx.vw("A3", 8192, BF16, 2048))
    ztok = None
    for et in range(H1):
        ztok = cx.dma("sync", lambda e, et=et: e.dma_start(
            out=z_sb[:, et * 2048:(et + 1) * 2048], in_=z_dram[et]), slot="zl2")
    for tc in range(4):
        for dt_ in range(DT):
            wb = wocol[dt_ % 3]
            wld = cx.dma("sync", lambda e, wb=wb, dt_=dt_: e.dma_start(
                out=wb.rearrange("p (h dm) -> p h dm", dm=128),
                in_=cx.dram["wo1_s"][:, :, dt_ * 128:(dt_ + 1) * 128]
                .rearrange("h emod dm -> emod h dm")), slot=("woc", dt_ % 3))
            pp = ps["ps_s"][:, (dt_ % 2) * 512:(dt_ % 2 + 1) * 512]
            for ec in range(H1):
                pl.op("tensor", lambda e, pp=pp, wb=wb, ec=ec, tc=tc: e.matmul(
                    out=pp, lhsT=wb[:, ec * 128:(ec + 1) * 128],
                    rhs=z_sb[:, ec * 2048 + tc * 512: ec * 2048 + (tc + 1) * 512],
                    start=(ec == 0), stop=(ec == H1 - 1)),
                    waits=[wld, ztok] +
                          (cx.use_wait(("psO", dt_ % 2)) if ec == 0 else []),
                    inc="te")
            mm_te = pl.v("te")
            cx.used(("woc", dt_ % 3), "te")
            rb = cx.vw("A3", 30720, F32, 512) if dt_ % 2 == 0 else \
                cx.vw("A3", 32768, F32, 512)
            rld = cx.dma("sync", lambda e, rb=rb, dt_=dt_, tc=tc: e.dma_start(
                out=rb, in_=r_in[dt_ * 128:(dt_ + 1) * 128,
                                 tc * 512:(tc + 1) * 512]), slot=("xf", dt_ % 2))
            pl.op("vector", lambda e, pp=pp, rb=rb, dt_=dt_: e.scalar_tensor_tensor(
                out=ych[:, dt_ * 512:(dt_ + 1) * 512], in0=pp,
                scalar=a4(cx, "boc")[:, dt_:dt_ + 1], in1=rb,
                op0=ALU.add, op1=ALU.add),
                waits=[("te", mm_te), rld] +
                      (cx.use_wait("lnych") if dt_ == 0 else []), inc="ve")
            cx.used(("xf", dt_ % 2), "ve")
            cx.used(("psO", dt_ % 2), "ve")
        y_ve = pl.v("ve")
        emit_ln_chunk(cx, tc, r_out, ych, ysq, y_ready=[("ve", y_ve)])
        cx.used("lnych", "te")


def emit_l1a(cx, seqs=(0, 1), prep=True, rin="r{s}_in", rout="r{s}_out"):
    if prep:
        emit_l1a_prep(cx)
        cx.pl.fence()
    for s in seqs:
        emit_l1a_seq(cx, s, cx.dram[rin.format(s=s)], cx.dram[rout.format(s=s)])
        cx.pl.fence()


def build_l1a_launch():
    cx = Ctx(bass.Bass("TRN2", target_bir_lowering=False))
    cx.dram_t("L1_attn_W_Q", [H1, D, E1], F32, kind="ExternalInput")
    cx.dram_t("L1_attn_W_K", [H1, D, E1], F32, kind="ExternalInput")
    cx.dram_t("L1_attn_W_V", [H1, D, E1], F32, kind="ExternalInput")
    cx.dram_t("L1_attn_W_O", [H1, E1, D], F32, kind="ExternalInput")
    cx.dram_t("L1_attn_b_Q", [H1, E1], F32, kind="ExternalInput")
    cx.dram_t("L1_attn_b_K", [H1, E1], F32, kind="ExternalInput")
    cx.dram_t("L1_attn_b_V", [H1, E1], F32, kind="ExternalInput")
    cx.dram_t("L1_attn_b_O", [D], F32, kind="ExternalInput")
    for s in range(NSEQ):
        cx.dram_t(f"r{s}_in", [D, T], F32, kind="ExternalInput")
        cx.dram_t(f"r{s}_out", [D, T], F32, kind="ExternalOutput")
    cx.dram_t("wq1_s", [H1, D, E1], BF16)
    cx.dram_t("wk1_s", [H1, D, E1], BF16)
    cx.dram_t("wv1_s", [H1, D, E1], BF16)
    cx.dram_t("wo1_s", [H1, E1, D], BF16)
    cx.dram_t("q_sp", [H1, 128, T], BF16)
    cx.dram_t("v_sp", [TT, 128, T], BF16)
    cx.dram_t("z_sp", [H1, 128, T], BF16)
    alloc_common(cx)
    load_consts(cx)
    emit_l1a(cx)
    return finalize(cx)



def emit_l0a_prep(cx):
    pl, ps = cx.pl, cx.psum
    emit_prep_transpose(cx, cx.dram["L0_attn_Wq"], D, D, cx.dram["wq0_s"],
                        ternarize=False, scale=0.125)
    cx.pl.fence()
    emit_prep_transpose(cx, cx.dram["L0_attn_Wk"], D, D, cx.dram["wk0_s"],
                        ternarize=False)
    cx.pl.fence()
    emit_prep_transpose(cx, cx.dram["L0_attn_Wv"], D, D, cx.dram["wv0_s"],
                        ternarize=False)
    cx.pl.fence()
    emit_prep_transpose(cx, cx.dram["L0_attn_Wo"], D, D, cx.dram["wo0_s"],
                        ternarize=False)
    cx.pl.fence()
    wfl = cx.vw("A3", 0, F32, 2048, parts=32)
    idf = a4(cx, "id_f32")
    ld = cx.dma("sync", lambda e, d=wfl: e.dma_start(
        out=d, in_=cx.dram["L0_attn_Wf"]), slot="pwc")
    for ct in range(16):
        pl.op("tensor", lambda e, ct=ct, d=wfl: e.transpose(
            out=ps["ps_z"][0:128, 0:32], in_=d[:, ct * 128:(ct + 1) * 128],
            identity=idf[0:32, 0:32]),
            waits=[ld, cx.state["consts"]] + cx.use_wait("ps_z"), inc="te")
        tr = pl.v("te")
        ev = cx.vw("A3", 20480, F32, 32)
        pl.op("scalar", lambda e, ev=ev: e.activation(
            out=ev, in_=ps["ps_z"][0:128, 0:32], func=AF.Copy),
            waits=[("te", tr)] + cx.use_wait("wfev"), inc="se")
        cx.used("ps_z", "se")
        sv = pl.v("se")
        tok = cx.dma("gpsimd", lambda e, ct=ct, ev=ev: e.dma_start(
            out=cx.dram["wf0_s"][ct], in_=ev), slot="wfev", waits=[("se", sv)])
        cx.used("wfev", tok)
    cx.used("pwc", "te")
    bfv = cx.dram["L0_attn_bf"].rearrange("(a b) -> a b", b=1)
    cx.dma("sync", lambda e: e.dma_start(out=a4(cx, "bfc"), in_=bfv), slot="bfc")


def emit_l0a_seq(cx, s, r_in, r_out):
    pl, ps = cx.pl, cx.psum
    q_dram, v_dram, z_dram = cx.dram["q_sp"], cx.dram["v_sp"], cx.dram["z_sp0"]
    cpos = cx.dram["c_sp"]
    cs2 = cx.dram["cs2_sp"]
    xb = cx.vw("A1", 0, BF16, DT * T)
    kT = cx.vw("A2", 0, BF16, DT * T)
    xfs = (cx.vw("A3", 0, F32, 2048), cx.vw("A3", 8192, F32, 2048))
    zf = cx.vw("A3", 16384, F32, 2048, parts=32)
    u1 = cx.vw("A3", 24576, F32, 2048, parts=32)
    u2 = cx.vw("A3", 32768, F32, 2048, parts=32)
    wfT = cx.vw("A3", 45056, F32, 512)
    wfld = cx.dma("sync", lambda e, d=wfT: e.dma_start(
        out=d.rearrange("p (a b) -> p a b", b=32),
        in_=cx.dram["wf0_s"].rearrange("a p b -> p a b")), slot="wfld")
    bfld = cx.dma("sync", lambda e: e.dma_start(
        out=a4(cx, "bfc"),
        in_=cx.dram["L0_attn_bf"].rearrange("(a b) -> a b", b=1)), slot="bfc")
    for dt_ in range(DT):
        buf = xfs[dt_ % 2]
        ld = cx.dma("sync", lambda e, b=buf, dt_=dt_: e.dma_start(
            out=b, in_=r_in[dt_ * 128:(dt_ + 1) * 128, :]), slot=("xf", dt_ % 2))
        pl.op("vector", lambda e, b=buf, dt_=dt_: e.tensor_copy(
            out=xb[:, dt_ * 2048:(dt_ + 1) * 2048], in_=b),
            waits=[ld], inc="ve")
        cx.used(("xf", dt_ % 2), "ve")
        for tc in range(4):
            pl.op("tensor", lambda e, b=buf, dt_=dt_, tc=tc: e.matmul(
                out=ps["ps_s"][0:32, tc * 512:(tc + 1) * 512],
                lhsT=wfT[:, dt_ * 32:(dt_ + 1) * 32],
                rhs=b[:, tc * 512:(tc + 1) * 512],
                start=(dt_ == 0), stop=(dt_ == DT - 1)),
                waits=[ld, wfld] +
                      (cx.use_wait(("psf", tc)) if dt_ == 0 else []), inc="te")
        cx.used(("xf", dt_ % 2), "te")
    f_te = pl.v("te")
    xb_ve = pl.v("ve")
    pl.op("scalar", lambda e: e.activation(
        out=zf, in_=ps["ps_s"][0:32, 0:2048], func=AF.Identity,
        bias=a4(cx, "bfc")), waits=[("te", f_te), bfld], inc="se")
    for tc in range(4):
        cx.used(("psf", tc), "se")
    pl.op("scalar", lambda e: e.activation(out=u1, in_=zf, func=AF.Abs),
          inc="se")
    pl.op("scalar", lambda e: e.activation(out=u2, in_=u1, func=AF.Exp,
                                           scale=-1.0), inc="se")
    pl.op("scalar", lambda e: e.activation(out=u1, in_=u2, func=AF.Ln,
                                           bias=1.0), inc="se")
    ls_se = pl.v("se")
    pl.op("vector", lambda e: e.tensor_scalar(
        out=u2, in0=zf, scalar1=0.0, scalar2=None, op0=ALU.min),
        waits=[("se", ls_se)], inc="ve")
    pl.op("vector", lambda e: e.tensor_sub(out=u1, in0=u1, in1=u2), inc="ve")
    pl.op("vector", lambda e: e.memset(zf, 1.0), inc="ve")
    pl.op("vector", lambda e: e.tensor_tensor_scan(
        out=u2, data0=zf, data1=u1, initial=0.0, op0=ALU.mult, op1=ALU.add),
        inc="ve")
    sc_ve = pl.v("ve")
    ctok = cx.dma("gpsimd", lambda e: e.dma_start(out=cpos, in_=u2),
                  slot="cst0", waits=[("ve", sc_ve)])
    cx.pl.fence()
    wcols = (cx.vw("A3", 0, BF16, 2048), cx.vw("A3", 4096, BF16, 2048))
    wvall = cx.vw("A3", 8192, BF16, 16 * 512)
    qsp = cx.vw("A3", 34816, BF16, 2048)
    for which, scr in (("k", "wk0_s"), ("q", "wq0_s")):
        for et in range(DT):
            wb = wcols[et % 2]
            wld = cx.dma("sync", lambda e, wb=wb, et=et, scr=scr: e.dma_start(
                out=wb, in_=cx.dram[scr][et]), slot=("wc", et % 2))
            for tc in range(4):
                pp = ps["ps_s"][:, (tc % 2) * 512:(tc % 2 + 1) * 512]
                for db in range(DT):
                    pl.op("tensor", lambda e, pp=pp, wb=wb, db=db, tc=tc: e.matmul(
                        out=pp, lhsT=wb[:, db * 128:(db + 1) * 128],
                        rhs=xb[:, db * 2048 + tc * 512: db * 2048 + (tc + 1) * 512],
                        start=(db == 0), stop=(db == DT - 1)),
                        waits=[wld, ("ve", xb_ve)] +
                              (cx.use_wait(("psqk", tc % 2)) if db == 0 else []),
                        inc="te")
                mm_te = pl.v("te")
                if which == "k":
                    pl.op("scalar", lambda e, pp=pp, et=et, tc=tc: e.activation(
                        out=kT[:, et * 2048 + tc * 512: et * 2048 + (tc + 1) * 512],
                        in_=pp, func=AF.Copy), waits=[("te", mm_te)], inc="se")
                else:
                    pl.op("scalar", lambda e, pp=pp, tc=tc: e.activation(
                        out=qsp[:, tc * 512:(tc + 1) * 512], in_=pp, func=AF.Copy),
                        waits=[("te", mm_te)] + cx.use_wait("qspb"), inc="se")
                cx.used(("psqk", tc % 2), "se")
            cx.used(("wc", et % 2), "te")
            if which == "q":
                sv = pl.v("se")
                tok = cx.dma("gpsimd", lambda e, et=et: e.dma_start(
                    out=q_dram[et], in_=qsp), slot="qspst", waits=[("se", sv)])
                cx.used("qspb", tok)
                cx.used(("qspb", 0), tok)
                cx.used(("qspb", 1), tok)
    for ec in range(4):
        wvtok = None
        for db in range(DT):
            wvtok = cx.dma("sync", lambda e, db=db, ec=ec: e.dma_start(
                out=wvall[:, db * 512:(db + 1) * 512].rearrange(
                    "p (a em) -> p a em", em=128),
                in_=cx.dram["wv0_s"][4 * ec:4 * ec + 4, :,
                                     db * 128:(db + 1) * 128]
                .rearrange("et dmod em -> dmod et em")), slot="wvall")
        for tt in range(TT):
            pp = ps["ps_s"][:, 1024 + (tt % 2) * 512:1536 + (tt % 2) * 512]
            for db in range(DT):
                pl.op("tensor", lambda e, pp=pp, db=db, tt=tt: e.matmul(
                    out=pp,
                    lhsT=xb[:, db * 2048 + tt * 128: db * 2048 + tt * 128 + 128],
                    rhs=wvall[:, db * 512:(db + 1) * 512],
                    start=(db == 0), stop=(db == DT - 1)),
                    waits=[("ve", xb_ve), wvtok] +
                          (cx.use_wait(("psv", tt % 2)) if db == 0 else []),
                    inc="te")
            mm_te = pl.v("te")
            pl.op("scalar", lambda e, pp=pp, tt=tt: e.activation(
                out=qsp[:, (tt % 2) * 512:(tt % 2) * 512 + 512], in_=pp,
                func=AF.Copy),
                waits=[("te", mm_te)] + cx.use_wait(("qspb", tt % 2)), inc="se")
            cx.used(("psv", tt % 2), "se")
            sv = pl.v("se")
            tok = cx.dma("gpsimd", lambda e, tt=tt, ec=ec: e.dma_start(
                out=v_dram[tt, :, ec * 512:(ec + 1) * 512],
                in_=qsp[:, (tt % 2) * 512:(tt % 2) * 512 + 512]),
                slot=("qspst", tt % 2), waits=[("se", sv)])
            cx.used(("qspb", tt % 2), tok)
        cx.used("wvall", "te")
    cx.pl.fence()
    v_sb = cx.vw("A1", 0, BF16, DT * T)
    qhs = (cx.vw("A3", 0, BF16, 2048, parts=64),
           cx.vw("A3", 0, BF16, 2048, parts=64, base=64))
    zstage = cx.vw("A3", 4096, BF16, 512, parts=64)
    pns = (cx.vw("A3", 8192, BF16, 2048), cx.vw("A3", 12288, BF16, 2048))
    ptbuf = cx.vw("A3", 16384, BF16, 8192)
    pt3 = ptbuf.rearrange("p (a b) -> p a b", b=512)
    crow16 = cx.vw("A3", 32768, F32, 2048, parts=16)
    crow16sh = cx.vw("A3", 40960, F32, 2048, parts=16)
    crowhi = cx.vw("A3", 32768, BF16, 2048, parts=16)
    crowlo = cx.vw("A3", 36864, BF16, 2048, parts=16)
    augrows = (cx.vw("A3", 32768, BF16, 2048, parts=2),
               cx.vw("A3", 36864, BF16, 2048, parts=2))
    vtok = None
    for tt in range(TT):
        vtok = cx.dma("sync", lambda e, tt=tt: e.dma_start(
            out=v_sb[:, tt * 2048:(tt + 1) * 2048], in_=v_dram[tt]), slot="vld")
    negms = (a4(cx, "negm"), a4(cx, "negm1"))
    sumss = (a4(cx, "sums"), a4(cx, "sums1"))
    recips = (a4(cx, "recipc"), a4(cx, "recipc1"))
    maskt = a4(cx, "maskt")
    ccol16 = a4(cx, "ccol16")
    onesrow = a4(cx, "onesrow_bf")
    ones2 = a4(cx, "ones2_bf")
    for h in range(H0):
        qh = qhs[h % 2]
        qld = cx.dma("sync", lambda e, h=h, qh=qh: e.dma_start(
            out=qh, in_=q_dram[h // 2][(h % 2) * 64:(h % 2) * 64 + 64, :]),
            slot=("qh", h % 2))
        c16ld = cx.dma("sync", lambda e, h=h: e.dma_start(
            out=crow16, in_=dram_bcast(cpos[h:h + 1, :], 16)), slot="c16",
            waits=[ctok] + cx.use_wait(("ar", 0)) + cx.use_wait(("ar", 1)))
        ccld = cx.dma("sync", lambda e, h=h: e.dma_start(
            out=ccol16, in_=AP(cpos.tensor, cpos.offset + h * 2048 + 127,
                               [[128, 16], [1, 1]]),
            allow_slow_non_contiguous=True), slot="ccol", waits=[ctok])
        pl.op("vector", lambda e: e.tensor_scalar(
            out=crow16sh, in0=crow16, scalar1=ccol16, scalar2=None,
            op0=ALU.subtract),
            waits=[c16ld, ccld] + cx.use_wait("c16sh") + cx.use_wait(("ar", 1)),
            inc="ve")
        pl.op("vector", lambda e: e.tensor_copy(out=crowhi, in_=crow16sh),
              waits=cx.use_wait("c16hi") + cx.use_wait(("ar", 0)), inc="ve")
        pl.op("vector", lambda e: e.tensor_sub(out=crow16sh, in0=crow16sh,
                                               in1=crowhi), inc="ve")
        pl.op("vector", lambda e: e.tensor_copy(out=crowlo, in_=crow16sh),
              waits=cx.use_wait("c16lo") + cx.use_wait(("ar", 1)), inc="ve")
        shv = pl.v("ve")
        cs2tok = cx.dma("gpsimd", lambda e, h=h: e.dma_start(
            out=cs2[0], in_=crowhi), slot="cs2", waits=[("ve", shv)])
        cs2tok = cx.dma("gpsimd", lambda e, h=h: e.dma_start(
            out=cs2[1], in_=crowlo), slot="cs2", waits=[("ve", shv)])
        cx.used("c16", "ve")
        cx.used("c16sh", cs2tok)
        cx.used("c16hi", cs2tok)
        cx.used("c16lo", cs2tok)
        pending = [None]

        def flush_l0(cx=cx, pl=pl, ps=ps, pending=pending, h=h):
            if pending[0] is None:
                return
            pn, ti, pnorm_ve = pending[0]
            pending[0] = None
            ti4 = ti % 4
            for tj in range(ti + 1):
                half = tj % 8
                pl.op("tensor", lambda e, pn=pn, tj=tj, half=half: e.transpose(
                    out=ps["ps_tp"][:, half * 128:(half + 1) * 128],
                    in_=pn[:, tj * 128:(tj + 1) * 128],
                    identity=a4(cx, "id_bf")),
                    waits=[pnorm_ve] +
                          (cx.use_wait("pstp") if half == 0 else []),
                    inc="te")
                if half == 7 or tj == ti:
                    t_te = pl.v("te")
                    n = half + 1
                    first = tj - half
                    pl.op("vector", lambda e, n=n, first=first, ti4=ti4:
                          e.tensor_copy(
                        out=pt3[:, first:first + n,
                                ti4 * 128:ti4 * 128 + 128],
                        in_=ps["ps_tp"][:, 0:n * 128].rearrange(
                            "p (a b) -> p a b", b=128)),
                        waits=[("te", t_te)], inc="ve")
                    cx.used("pstp", "ve")
            cx.used(("pn", ti4 % 2), "te")
            if ti4 == 3:
                ic = ti // 4
                ptb_ve = pl.v("ve")
                for tj in range(4 * (ic + 1)):
                    o = max(0, (tj - 4 * ic) * 128)
                    pl.op("tensor", lambda e, tj=tj, o=o, h=h, ic=ic: e.matmul(
                        out=ps["ps_z"][0:64, o:512],
                        lhsT=v_sb[:, tj * 2048 + h * 64:
                                  tj * 2048 + h * 64 + 64],
                        rhs=pt3[:, tj, o:512],
                        start=(tj == 0), stop=(tj == 4 * (ic + 1) - 1)),
                        waits=[("ve", ptb_ve), vtok] +
                              (cx.use_wait("psz") if tj == 0 else []), inc="te")
                av_te = pl.v("te")
                cx.used("ptbuf", "te")
                pl.op("scalar", lambda e: e.activation(
                    out=zstage, in_=ps["ps_z"][0:64, 0:512], func=AF.Copy),
                    waits=[("te", av_te)] + cx.use_wait("zst"), inc="se")
                cx.used("psz", "se")
                zv = pl.v("se")
                tok = cx.dma("gpsimd", lambda e, h=h, ic=ic: e.dma_start(
                    out=z_dram[h * 64:h * 64 + 64, ic * 512:(ic + 1) * 512],
                    in_=zstage), slot="zst", waits=[("se", zv)])
                cx.used("zst", tok)

        for ic in range(4):
            for ti4 in range(4):
                ti = 4 * ic + ti4
                ji = 128 * (ti + 1)
                pn = pns[ti4 % 2]
                par = ti % 2
                negm, sums, recipc = negms[par], sumss[par], recips[par]
                ar = augrows[ti % 2]
                artok = cx.dma("sync", lambda e, ti=ti, ji=ji, ar=ar: e.dma_start(
                    out=ar[0:2, 0:ji], in_=cs2[:, ti, 0:ji]),
                    slot=("ar", ti % 2), waits=[cs2tok])
                nb = (ji + 511) // 512
                pmax = a4(cx, "pmax0") if par == 0 else a4(cx, "pmax1")
                for jc in range(nb):
                    n = min(512, ji - jc * 512)
                    pl.op("tensor", lambda e, qh=qh, h=h, jc=jc, n=n, ti=ti:
                          e.matmul(
                        out=ps["ps_s"][:, jc * 512:jc * 512 + n],
                        lhsT=qh[:, ti * 128:(ti + 1) * 128],
                        rhs=kT[(h % 2) * 64:(h % 2) * 64 + 64,
                               (h // 2) * 2048 + jc * 512:
                               (h // 2) * 2048 + jc * 512 + n],
                        start=True, stop=False),
                        waits=[qld] + cx.use_wait(("psS", jc)), inc="te")
                    pl.op("tensor", lambda e, jc=jc, n=n, ar=ar: e.matmul(
                        out=ps["ps_s"][:, jc * 512:jc * 512 + n],
                        lhsT=ones2, rhs=ar[0:2, jc * 512:jc * 512 + n],
                        start=False, stop=True),
                        waits=[artok, cx.state["consts"]], inc="te")
                    s_te = pl.v("te")
                    if jc == nb - 1:
                        dji = ji - 128
                        pl.op("vector", lambda e, dji=dji: e.tensor_tensor(
                            out=ps["ps_s"][:, dji:dji + 128],
                            in0=ps["ps_s"][:, dji:dji + 128],
                            in1=maskt, op=ALU.add),
                            waits=[("te", s_te), cx.state["consts"]] +
                                  cx.use_wait(("negm", par)), inc="ve")
                    pl.op("vector", lambda e, jc=jc, n=n, pmax=pmax:
                          e.tensor_reduce(
                        out=pmax[:, jc:jc + 1],
                        in_=ps["ps_s"][:, jc * 512:jc * 512 + n],
                        axis=AX.X, op=ALU.max),
                        waits=[("te", s_te)] +
                              (cx.use_wait(("sums", par)) if jc == 0 else []),
                        inc="ve")
                cx.used(("ar", ti % 2), "te")
                pl.op("vector", lambda e, negm=negm, pmax=pmax, nb=nb:
                      e.tensor_reduce(
                    out=negm, in_=pmax[:, 0:nb], axis=AX.X, op=ALU.max,
                    negate=True), waits=cx.use_wait(("negm", par)), inc="ve")
                mx_ve = pl.v("ve")
                pl.op("scalar", lambda e, pn=pn, ji=ji, negm=negm, sums=sums:
                      e.activation(
                    out=pn[:, 0:ji], in_=ps["ps_s"][:, 0:ji], func=AF.Exp,
                    bias=negm, accum_out=sums),
                    waits=[("ve", mx_ve)] + cx.use_wait(("pn", ti4 % 2)),
                    inc="se")
                e_se = pl.v("se")
                cx.used(("negm", par), "se")
                for jc in range((ji + 511) // 512):
                    cx.used(("psS", jc), "se")
                pl.op("vector", lambda e, recipc=recipc, sums=sums:
                      e.reciprocal(out=recipc, in_=sums),
                      waits=[("se", e_se)] + cx.use_wait(("recip", par)),
                      inc="ve")
                cx.used(("sums", par), "ve")
                rc_ve = pl.v("ve")
                pl.op("scalar", lambda e, pn=pn, ji=ji, recipc=recipc:
                      e.activation(
                    out=pn[:, 0:ji], in_=pn[:, 0:ji], func=AF.Identity,
                    scale=recipc), waits=[("ve", rc_ve)], inc="se")
                cx.used(("recip", par), "se")
                pnorm_ve = ("se", pl.v("se"))
                flush_l0()
                pending[0] = (pn, ti, pnorm_ve)
        flush_l0()
        cx.used(("qh", h % 2), "te")
    cx.pl.fence()
    z_sb = cx.vw("A1", 0, BF16, DT * T)
    ych = cx.vw("A2", 0, F32, 8192)
    ysq = cx.vw("A2", 32768, BF16, 8192)
    wocol = (cx.vw("A3", 0, BF16, 2048), cx.vw("A3", 4096, BF16, 2048),
             cx.vw("A3", 8192, BF16, 2048))
    rbs = (cx.vw("A3", 30720, F32, 512), cx.vw("A3", 32768, F32, 512))
    ztok = None
    for et in range(DT):
        ztok = cx.dma("sync", lambda e, et=et: e.dma_start(
            out=z_sb[:, et * 2048:(et + 1) * 2048],
            in_=z_dram[et * 128:(et + 1) * 128, :]), slot="zl2")
    for tc in range(4):
        for dt_ in range(DT):
            wb = wocol[dt_ % 3]
            wld = cx.dma("sync", lambda e, wb=wb, dt_=dt_: e.dma_start(
                out=wb, in_=cx.dram["wo0_s"][dt_]), slot=("woc", dt_ % 3))
            pp = ps["ps_s"][:, (dt_ % 2) * 512:(dt_ % 2 + 1) * 512]
            for ec in range(DT):
                pl.op("tensor", lambda e, pp=pp, wb=wb, ec=ec, tc=tc: e.matmul(
                    out=pp, lhsT=wb[:, ec * 128:(ec + 1) * 128],
                    rhs=z_sb[:, ec * 2048 + tc * 512: ec * 2048 + (tc + 1) * 512],
                    start=(ec == 0), stop=(ec == DT - 1)),
                    waits=[wld, ztok] +
                          (cx.use_wait(("psO", dt_ % 2)) if ec == 0 else []),
                    inc="te")
            mm_te = pl.v("te")
            cx.used(("woc", dt_ % 3), "te")
            rb = rbs[dt_ % 2]
            rld = cx.dma("sync", lambda e, rb=rb, dt_=dt_, tc=tc: e.dma_start(
                out=rb, in_=r_in[dt_ * 128:(dt_ + 1) * 128,
                                 tc * 512:(tc + 1) * 512]), slot=("xf", dt_ % 2))
            pl.op("vector", lambda e, pp=pp, rb=rb, dt_=dt_: e.tensor_add(
                out=ych[:, dt_ * 512:(dt_ + 1) * 512], in0=pp, in1=rb),
                waits=[("te", mm_te), rld] +
                      (cx.use_wait("lnych") if dt_ == 0 else []), inc="ve")
            cx.used(("xf", dt_ % 2), "ve")
            cx.used(("psO", dt_ % 2), "ve")
        y_ve = pl.v("ve")
        emit_ln_chunk(cx, tc, r_out, ych, ysq, y_ready=[("ve", y_ve)])
        cx.used("lnych", "te")


def emit_l0a(cx, seqs=(0, 1), prep=True, rin="r{s}_in", rout="r{s}_out"):
    if prep:
        emit_l0a_prep(cx)
        cx.pl.fence()
    for s in seqs:
        emit_l0a_seq(cx, s, cx.dram[rin.format(s=s)], cx.dram[rout.format(s=s)])
        cx.pl.fence()


def build_l0a_launch():
    cx = Ctx(bass.Bass("TRN2", target_bir_lowering=False))
    cx.dram_t("L0_attn_Wq", [D, D], F32, kind="ExternalInput")
    cx.dram_t("L0_attn_Wk", [D, D], F32, kind="ExternalInput")
    cx.dram_t("L0_attn_Wv", [D, D], F32, kind="ExternalInput")
    cx.dram_t("L0_attn_Wf", [H0, D], F32, kind="ExternalInput")
    cx.dram_t("L0_attn_bf", [H0], F32, kind="ExternalInput")
    cx.dram_t("L0_attn_Wo", [D, D], F32, kind="ExternalInput")
    for s in range(NSEQ):
        cx.dram_t(f"r{s}_in", [D, T], F32, kind="ExternalInput")
        cx.dram_t(f"r{s}_out", [D, T], F32, kind="ExternalOutput")
    for nm in ("wq0_s", "wk0_s", "wv0_s", "wo0_s"):
        cx.dram_t(nm, [DT, 128, D], BF16)
    cx.dram_t("wf0_s", [16, 128, 32], F32)
    cx.dram_t("c_sp", [H0, T], F32)
    cx.dram_t("cs2_sp", [2, 16, T], BF16)
    cx.dram_t("q_sp", [DT, 128, T], BF16)
    cx.dram_t("v_sp", [TT, 128, T], BF16)
    cx.dram_t("z_sp0", [D, T], BF16)
    alloc_common(cx)
    load_consts(cx)
    emit_l0a(cx)
    return finalize(cx)



def finalize(cx, self_waits=False):
    nc = cx.nc
    waits = [(s, cx.pl.v(s)) for s in cx.pl.cnt if s.startswith("dm")]
    cx.pl.op("gpsimd", lambda e: e.engine_nop(), waits=waits)
    sems = {}
    cms = []
    names = set(cx.pl.cnt.keys())
    for _, _, ws, _ in cx.pl.steps:
        names.update(s for s, _ in ws)
    for s in sorted(names):
        cm = nc.semaphore(s)
        sems[s] = cm.__enter__()
        cms.append(cm)
    cx.pl.emit(nc, sems, self_waits=self_waits)
    for cm in reversed(cms):
        cm.__exit__(None, None, None)
    return nc


def build_l0f_launch():
    cx = Ctx(bass.Bass("TRN2", target_bir_lowering=False))
    cx.dram_t("L0_ffn_w_fc", [HID0, D], F32, kind="ExternalInput")
    cx.dram_t("L0_ffn_w_proj", [D, HID0], F32, kind="ExternalInput")
    for s in range(NSEQ):
        cx.dram_t(f"r{s}_in", [D, T], F32, kind="ExternalInput")
        cx.dram_t(f"r{s}_out", [D, T], F32, kind="ExternalOutput")
    cx.dram_t("wfcT_s", [HT0, 128, D], BF16)
    cx.dram_t("wprT_s", [DT, 128, HID0], BF16)
    cx.dram_t("h_sp", [4, HT0, 128, 512], BF16)
    alloc_common(cx)
    load_consts(cx)
    emit_l0f(cx)
    return finalize(cx)


def build_full_launch():
    cx = Ctx(bass.Bass("TRN2", target_bir_lowering=False))
    cx.dram_t("L0_attn_Wq", [D, D], F32, kind="ExternalInput")
    cx.dram_t("L0_attn_Wk", [D, D], F32, kind="ExternalInput")
    cx.dram_t("L0_attn_Wv", [D, D], F32, kind="ExternalInput")
    cx.dram_t("L0_attn_Wf", [H0, D], F32, kind="ExternalInput")
    cx.dram_t("L0_attn_bf", [H0], F32, kind="ExternalInput")
    cx.dram_t("L0_attn_Wo", [D, D], F32, kind="ExternalInput")
    cx.dram_t("L0_ffn_w_fc", [HID0, D], F32, kind="ExternalInput")
    cx.dram_t("L0_ffn_w_proj", [D, HID0], F32, kind="ExternalInput")
    cx.dram_t("L1_attn_W_Q", [H1, D, E1], F32, kind="ExternalInput")
    cx.dram_t("L1_attn_W_K", [H1, D, E1], F32, kind="ExternalInput")
    cx.dram_t("L1_attn_W_V", [H1, D, E1], F32, kind="ExternalInput")
    cx.dram_t("L1_attn_W_O", [H1, E1, D], F32, kind="ExternalInput")
    cx.dram_t("L1_attn_b_Q", [H1, E1], F32, kind="ExternalInput")
    cx.dram_t("L1_attn_b_K", [H1, E1], F32, kind="ExternalInput")
    cx.dram_t("L1_attn_b_V", [H1, E1], F32, kind="ExternalInput")
    cx.dram_t("L1_attn_b_O", [D], F32, kind="ExternalInput")
    cx.dram_t("L1_ffn_w_gate", [D, HID1], I8, kind="ExternalInput")
    cx.dram_t("L1_ffn_w_up", [D, HID1], I8, kind="ExternalInput")
    cx.dram_t("L1_ffn_w_down", [HID1, D], I8, kind="ExternalInput")
    cx.dram_t("L1_ffn_s_gate", [HID1], F32, kind="ExternalInput")
    cx.dram_t("L1_ffn_s_up", [HID1], F32, kind="ExternalInput")
    cx.dram_t("L1_ffn_s_down", [D], F32, kind="ExternalInput")
    for s in range(NSEQ):
        cx.dram_t(f"r{s}_in", [D, T], F32, kind="ExternalInput")
        cx.dram_t(f"r{s}_out", [D, T], F32, kind="ExternalOutput")
        cx.dram_t(f"r{s}_a", [D, T], F32)
        cx.dram_t(f"r{s}_b", [D, T], F32)
        cx.dram_t(f"r{s}_c", [D, T], F32)
    for nm in ("wq0_s", "wk0_s", "wv0_s", "wo0_s"):
        cx.dram_t(nm, [DT, 128, D], BF16)
    cx.dram_t("wf0_s", [16, 128, 32], F32)
    cx.dram_t("c_sp", [H0, T], F32)
    cx.dram_t("cs2_sp", [2, 16, T], BF16)
    cx.dram_t("q_sp", [DT, 128, T], BF16)
    cx.dram_t("v_sp", [TT, 128, T], BF16)
    cx.dram_t("z_sp0", [D, T], BF16)
    cx.dram_t("z_sp", [H1, 128, T], BF16)
    cx.dram_t("wfcT_s", [HT0, 128, D], BF16)
    cx.dram_t("wprT_s", [DT, 128, HID0], BF16)
    cx.dram_t("h_sp", [4, HT0, 128, 512], BF16)
    cx.dram_t("wq1_s", [H1, D, E1], BF16)
    cx.dram_t("wk1_s", [H1, D, E1], BF16)
    cx.dram_t("wv1_s", [H1, D, E1], BF16)
    cx.dram_t("wo1_s", [H1, E1, D], BF16)
    cx.dram_t("wguT_s", [HT1, 128, 2 * D], BF16)
    cx.dram_t("wdT_s", [DT, 128, HID1], BF16)
    cx.dram_t("scales_s", [128, 256], F32)
    cx.dram_t("fuse_sp", [4, HT1, 128, 512], BF16)
    alloc_common(cx)
    load_consts(cx)
    emit_l0a(cx, rin="r{s}_in", rout="r{s}_a")
    emit_l0f(cx, rin="r{s}_a", rout="r{s}_b")
    emit_l1a(cx, rin="r{s}_b", rout="r{s}_c")
    emit_l1f(cx, rin="r{s}_c", rout="r{s}_out")
    return finalize(cx)


_BUILD_CACHE = {}


def _get_nc(name, builder):
    if name not in _BUILD_CACHE:
        _BUILD_CACHE[name] = builder()
    return _BUILD_CACHE[name]


def kernel(**inputs):
    ii = {k: np.asarray(v) for k, v in inputs.items()}
    nc = _get_nc("full", build_full_launch)
    wkeys = [k for k in ii if k != "L0_attn_x"]
    in_maps = []
    for c in range(NCORES):
        m = {k: ii[k] for k in wkeys}
        m["r0_in"] = np.ascontiguousarray(ii["L0_attn_x"][2 * c].T)
        m["r1_in"] = np.ascontiguousarray(ii["L0_attn_x"][2 * c + 1].T)
        in_maps.append(m)
    res = run_bass_kernel_spmd(nc, in_maps, core_ids=list(range(NCORES)))
    out = np.empty((16, T, D), dtype=np.float32)
    for c in range(NCORES):
        for s in range(2):
            out[2 * c + s] = np.asarray(res.results[c][f"r{s}_out"]).T
    return out
